# version 20
# baseline (speedup 1.0000x reference)
import numpy as np
import ml_dtypes
from contextlib import ExitStack

import concourse.bass as bass
import concourse.mybir as mybir
from concourse.bass_utils import run_bass_kernel_spmd

F32 = mybir.dt.float32
BF16 = mybir.dt.bfloat16
I32 = mybir.dt.int32
AF = mybir.ActivationFunctionType
ALU = mybir.AluOpType
AX = mybir.AxisListType

E = 1024
NCH = 8
H = 16
DH = 64
RANKS = 4
BW = 512
KT = 128
LN8 = 2.0794415416798357


def build_nc(TOK=2048, QB=512, phases=9, att_n=10**9):
    S = RANKS * TOK
    COLS = TOK + 2
    NQB = S // QB
    QBW = TOK // QB
    assert TOK % QB == 0 and QB % KT == 0 and 128 <= QB <= 512

    nc = bass.Bass("TRN2", target_bir_lowering=False)

    xin = nc.dram_tensor("xin", [E, COLS], F32, kind="ExternalInput").ap()
    wrT = nc.dram_tensor("wrT", [E, E], F32, kind="ExternalInput").ap()
    wkT = nc.dram_tensor("wkT", [E, E], F32, kind="ExternalInput").ap()
    wvT = nc.dram_tensor("wvT", [E, E], F32, kind="ExternalInput").ap()
    wqaT = nc.dram_tensor("wqaT", [E, E], F32, kind="ExternalInput").ap()
    wkaT = nc.dram_tensor("wkaT", [E, E], F32, kind="ExternalInput").ap()
    wvaT = nc.dram_tensor("wvaT", [E, E], F32, kind="ExternalInput").ap()
    wpr = nc.dram_tensor("wpr", [E, E], F32, kind="ExternalInput").ap()
    bq_t = nc.dram_tensor("bq_t", [128, 8], F32, kind="ExternalInput").ap()
    bk_t = nc.dram_tensor("bk_t", [128, 8], F32, kind="ExternalInput").ap()
    bv_bc = nc.dram_tensor("bv_bc", [128, E], F32, kind="ExternalInput").ap()
    mur_t = nc.dram_tensor("mur_t", [128, 8], F32, kind="ExternalInput").ap()
    muk_t = nc.dram_tensor("muk_t", [128, 8], F32, kind="ExternalInput").ap()
    hmask = nc.dram_tensor("hmask", [128, 1], F32, kind="ExternalInput").ap()
    id128 = nc.dram_tensor("id128", [128, 128], F32, kind="ExternalInput").ap()
    onesr = nc.dram_tensor("onesr", [1, 128], F32, kind="ExternalInput").ap()
    onescb = nc.dram_tensor("onescb", [128, 1], BF16, kind="ExternalInput").ap()
    consts = nc.dram_tensor("consts", [1, 4], F32, kind="ExternalInput").ap()
    rnk = nc.dram_tensor("rnk", [1, 1], I32, kind="ExternalInput").ap()
    out = nc.dram_tensor("out", [TOK, E], F32, kind="ExternalOutput").ap()

    qa_bf = nc.dram_tensor("qa_bf", [E, E], BF16).ap()
    ka_bf = nc.dram_tensor("ka_bf", [E, E], BF16).ap()
    va_bf = nc.dram_tensor("va_bf", [E, E], BF16).ap()


    z_dram = nc.dram_tensor("z_dram", [256, S], BF16).ap()
    stage_x2h = nc.dram_tensor("stage_x2h", [8, 128], F32).ap()
    g_x2h_t = nc.dram_tensor("g_x2h", [RANKS * 8, 128], F32)
    myx2h = nc.dram_tensor("myx2h", [8, 128], F32).ap()
    stage_nrm = nc.dram_tensor("stage_nrm", [1, 2], F32).ap()
    g_nrm_t = nc.dram_tensor("g_nrm", [RANKS, 2], F32)


    def blocks(lo):
        out_ = []
        c = lo
        while c < COLS:
            if c < 2:
                w = 2 - c
            else:
                w = min(BW - ((c - 2) % BW), COLS - c)
            out_.append((c, w))
            c += w
        return out_

    LN1B = blocks(0)
    FFN1B = blocks(1)
    LN2B = blocks(2)
    LN3B = blocks(1)
    stage_qk_bt = [nc.dram_tensor(f"stage_qk{b}", [RANKS * 512, w], BF16)
                   for b, (c, w) in enumerate(LN2B)]
    g_qk_bt = [nc.dram_tensor(f"g_qk{b}", [RANKS * RANKS * 512, w], BF16)
               for b, (c, w) in enumerate(LN2B)]
    stage_v_bt = [nc.dram_tensor(f"stage_v{b}", [RANKS * w, 256], BF16)
                  for b, (c, w) in enumerate(LN2B)]
    g_v_bt = [nc.dram_tensor(f"g_v{b}", [RANKS * RANKS * w, 256], BF16)
              for b, (c, w) in enumerate(LN2B)]
    myqk_bt = [nc.dram_tensor(f"myqk{b}", [RANKS * 512, w], BF16)
               for b, (c, w) in enumerate(LN2B)]
    myv_bt = [nc.dram_tensor(f"myv{b}", [RANKS * w, 256], BF16)
              for b, (c, w) in enumerate(LN2B)]
    NB2 = len(LN2B)

    with ExitStack() as ctx:
        en = ctx.enter_context
        xs = en(nc.sbuf_tensor("xs", [128, NCH * COLS], F32))
        ysb = en(nc.sbuf_tensor("ysb", [128, NCH * BW], BF16))
        ylast = en(nc.sbuf_tensor("ylast", [128, NCH], BF16))
        scr32 = en(nc.sbuf_tensor("scr32", [128, BW], F32))
        xrb = en(nc.sbuf_tensor("xrb", [128, NCH * BW], BF16))
        xkb = en(nc.sbuf_tensor("xkb", [128, NCH * BW], BF16))
        rscr = en(nc.sbuf_tensor("rscr", [128, 2 * BW], BF16))
        kact = en(nc.sbuf_tensor("kact", [128, NCH * BW], BF16))
        wr = en(nc.sbuf_tensor("wr", [128, NCH * E], BF16))
        wk = en(nc.sbuf_tensor("wk", [128, NCH * E], BF16))
        wv = en(nc.sbuf_tensor("wv", [128, NCH * E], BF16))
        qkvw = en(nc.sbuf_tensor("qkvw", [128, 2 * 3 * NCH * 128], BF16))
        projw = en(nc.sbuf_tensor("projw", [128, NCH * E], BF16))
        qblk = en(nc.sbuf_tensor("qblk", [128, 2 * QB], BF16))
        kvbuf = en(nc.sbuf_tensor("kvbuf", [128, 4 * KT + 4 * 144], BF16))
        pbuf = en(nc.sbuf_tensor("pbuf", [128, 2 * 2 * QB], BF16))
        zbf = en(nc.sbuf_tensor("zbf", [64, 2 * QB], BF16))
        scale_sb = en(nc.sbuf_tensor("scale_sb", [64, 2 * QB], F32))
        sqt = en(nc.sbuf_tensor("sqt", [128, BW], BF16))
        outsb = en(nc.sbuf_tensor("outsb", [128, 512], F32))
        sbias_q = en(nc.sbuf_tensor("sbias_q", [128, 8], F32))
        sbias_k = en(nc.sbuf_tensor("sbias_k", [128, 8], F32))
        sbv8 = en(nc.sbuf_tensor("sbv8", [128, E], BF16))
        smur = en(nc.sbuf_tensor("smur", [128, 8], F32))
        smuk = en(nc.sbuf_tensor("smuk", [128, 8], F32))
        shm = en(nc.sbuf_tensor("shm", [128, 1], F32))
        sid = en(nc.sbuf_tensor("sid", [128, 128], F32))
        sonesr = en(nc.sbuf_tensor("sonesr", [1, 128], F32))
        sonescb = en(nc.sbuf_tensor("sonescb", [128, 1], BF16))
        sconsts = en(nc.sbuf_tensor("sconsts", [1, 4], F32))
        srnk = en(nc.sbuf_tensor("srnk", [1, 1], I32))
        m_sb = en(nc.sbuf_tensor("m_sb", [1, BW], F32))
        v_sb = en(nc.sbuf_tensor("v_sb", [1, BW], F32))
        v2_sb = en(nc.sbuf_tensor("v2_sb", [1, BW], F32))
        rstd_sb = en(nc.sbuf_tensor("rstd_sb", [1, BW], F32))
        qkmax = en(nc.sbuf_tensor("qkmax", [1, 2], F32))
        nrm_s = en(nc.sbuf_tensor("nrm_s", [1, 1], F32))
        gnrm_sb = en(nc.sbuf_tensor("gnrm_sb", [1, RANKS * 2], F32))
        qkg = en(nc.sbuf_tensor("qkg", [1, 2], F32))
        qkg2 = en(nc.sbuf_tensor("qkg2", [1, 2], F32))
        negb = en(nc.sbuf_tensor("negb", [128, 1], F32))
        nb_s = en(nc.sbuf_tensor("nb_s", [1, 1], F32))

        PS = en(nc.psum_tensor("PS", [128, 4096], F32))

        class Ctr:
            def __init__(self, name, step=1):
                self.sem = en(nc.semaphore(name))
                self.step = step
                self.v = 0

            def bump(self):
                self.v += self.step
                return self.v

        s_ld = Ctr("s_ld", 16)
        s_wffn = Ctr("s_wffn", 16)
        s_wqc = Ctr("s_wqc", 16)
        s_wpj = Ctr("s_wpj", 16)
        s_wq = [Ctr("s_wq0", 16), Ctr("s_wq1", 16)]
        s_stqp = [Ctr(f"s_stq{i}", 16) for i in range(6)]
        s_qbp = [Ctr("s_qb0", 16), Ctr("s_qb1", 16)]
        s_kv = [Ctr(f"s_kv{i}", 16) for i in range(4)]
        s_zst = Ctr("s_zst", 16)
        s_gn = Ctr("s_gn", 16)
        s_gz = Ctr("s_gz", 16)
        s_out = Ctr("s_out", 16)
        s_cp = Ctr("s_cp", 16)
        s_bv = Ctr("s_bv", 16)
        cpe = Ctr("cpe")
        cac = Ctr("cac")
        cdv = Ctr("cdv")
        cpl = Ctr("cpl")
        ccc = Ctr("ccc")
        CTRS = {"pe": cpe, "ac": cac, "dv": cdv, "pl": cpl}

        block = en(nc.Block())
        ops = {"sy": [], "gp": [], "pe": [], "ac": [], "dv": []}

        def emit_all():
            _build_emit(block, ops)

        def on(q, fn):
            ops[q].append(fn)

        def wait(q, ctr, val):
            if val and val > 0:
                on(q, lambda EG, s=ctr.sem, v=val: EG.wait_ge(s, v))

        def dma(q, c, out_ap, in_ap, **kw):
            eq = {"sy": "sy", "gp": "gp", "sc": "ac", "ac2": "ac",
                  "vd": "dv"}[q]
            c.bump()
            on(eq, lambda EG, o=out_ap, i=in_ap, s=c.sem, kw=kw:
               EG.dma_start(out=o, in_=i, **kw).then_inc(s, 16))
            return c.v

        def mm(out_ap, lhsT, rhs, start, stop, tp=None, inc=False):
            if inc:
                cpe.bump()
            v = cpe.v

            def f(T, o=out_ap, l=lhsT, r=rhs, st=start, sp=stop, tp=tp,
                  inc=inc):
                i = T.matmul(o, l, r, start=st, stop=sp, tile_position=tp)
                if inc:
                    i.then_inc(cpe.sem, 1)
            on("pe", f)
            return v

        def trans(out_ap, in_ap, inc=False):
            if inc:
                cpe.bump()
            v = cpe.v

            def f(T, o=out_ap, i=in_ap, inc=inc):
                r = T.transpose(o, i, sid[:, :])
                if inc:
                    r.then_inc(cpe.sem, 1)
            on("pe", f)
            return v

        def act(out_ap, in_ap, func, bias=0.0, scale=1.0, chain=True):
            if chain:
                wait("ac", cac, cac.v)
            v = cac.bump()

            def f(A, o=out_ap, i=in_ap, fn=func, b=bias, s=scale):
                A.activation(o, i, fn, bias=b, scale=s).then_inc(cac.sem, 1)
            on("ac", f)
            return v

        def dv(name, *args, chain=True, **kw):
            if chain:
                wait("dv", cdv, cdv.v)
            v = cdv.bump()

            def f(V, n=name, a=args, kw=kw):
                getattr(V, n)(*a, **kw).then_inc(cdv.sem, 1)
            on("dv", f)
            return v

        def pl(name, *args, **kw):
            wait("gp", cpl, cpl.v)
            v = cpl.bump()

            def f(G, n=name, a=args, kw=kw):
                getattr(G, n)(*a, **kw).then_inc(cpl.sem, 1)
            on("gp", f)
            return v

        def bank(i, w=512, rows=(0, 128)):
            return PS[rows[0]:rows[1], 512 * i: 512 * i + w]

        last_bank_use = {i: ("dv", 0) for i in range(8)}

        def use_bank(i, who, val):
            last_bank_use[i] = (who, val)

        def wait_bank(q, i):
            who, val = last_bank_use[i]
            wait(q, CTRS[who], val)

        v_x = dma("sy", s_ld, xs[:, :].rearrange("p (c t) -> p c t", c=NCH),
                  xin.rearrange("(c p) t -> p c t", p=128))
        for dst, src in [(sbias_q[:, :], bq_t), (sbias_k[:, :], bk_t),
                         (smur[:, :], mur_t),
                         (smuk[:, :], muk_t), (shm[:, :], hmask),
                         (sid[:, :], id128), (sonesr[:, :], onesr),
                         (sonescb[:, :], onescb), (sconsts[:, :], consts),
                         (srnk[:, :], rnk)]:
            v_small = dma("sy", s_ld, dst, src)
        v_wffn = 0
        for dst, src in [(wr, wrT), (wk, wkT), (wv, wvT)]:
            v_wffn = dma("gp", s_wffn,
                         dst[:, :].rearrange("p (c d) -> p c d", c=NCH),
                         src.rearrange("(c p) d -> p c d", p=128))
        v_qc = 0
        for dst, src in [(qa_bf, wqaT), (ka_bf, wkaT), (va_bf, wvaT)]:
            for hh in range(8):
                v_qc = dma("gp", s_wqc,
                           dst[hh * 128:(hh + 1) * 128, :]
                           .rearrange("p (c d) -> p c d", d=128),
                           src[:, 128 * hh:128 * (hh + 1)]
                           .rearrange("(c p) d -> p c d", p=128))
        v_wpr = dma("gp", s_wpj,
                    projw[:, :].rearrange("p (c d) -> p c d", c=NCH),
                    wpr.rearrange("(c p) d -> p c d", p=128))
        v_bv = dma("gp", s_bv, sbv8[:, :], bv_bc)
        wait("dv", s_ld, v_small)
        wait("dv", s_bv, v_bv)
        v_bv8 = dv("tensor_scalar_mul", sbv8[:, :], sbv8[:, :], 0.125)

        st_ = {"ln_apply": 0,
               "ysb_pe": 0,
               "ffn_pe": 0,
               "ffn_rscr": 0,
               "sq_ac": 0,
               "sqt_pe": 0,
               "lnscr_pe": 0,
               "xs_dv": 0}

        def emit_ln_stats(cs, wd, yb=None, ystride=BW):
            yb = ysb if yb is None else yb
            v_sum = 0
            for c in range(NCH):
                xc = xs[:, c * COLS + cs: c * COLS + cs + wd]
                if c == 0:
                    wait("dv", cdv, st_["ln_apply"])
                    wait("dv", cpe, st_["ysb_pe"])
                    wait("ac", cdv, st_["xs_dv"])
                vcp = dv("tensor_copy", yb[:, c * ystride: c * ystride + wd],
                         xc)
                wait("ac", cpe, st_["sqt_pe"])
                vsq = act(sqt[:, 0:wd], xc, AF.Square)
                if c == 0:
                    wait_bank("pe", 0)
                    wait_bank("pe", 1)
                wait("pe", cdv, vcp)
                mm(bank(0, wd, (0, 1)), sonescb[:, :],
                   yb[:, c * ystride: c * ystride + wd], c == 0,
                   c == NCH - 1)
                wait("pe", cac, vsq)
                v_sum = mm(bank(1, wd, (0, 1)), sonescb[:, :], sqt[:, 0:wd],
                           c == 0, c == NCH - 1, inc=True)
                st_["sqt_pe"] = v_sum
            st_["sq_ac"] = vsq
            st_["ysb_pe"] = v_sum
            return v_sum

        def emit_ln_rstd(cs, wd, v_sum):
            wait("dv", cpe, max(v_sum, st_["lnscr_pe"]))
            vm = dv("tensor_scalar_mul", m_sb[0:1, 0:wd],
                    bank(0, wd, (0, 1)), 1.0 / E)
            ve1 = dv("tensor_scalar_mul", v_sb[0:1, 0:wd],
                     bank(1, wd, (0, 1)), 1.0 / E)
            use_bank(0, "dv", vm)
            use_bank(1, "dv", ve1)
            vm2 = dv("tensor_mul", v2_sb[0:1, 0:wd], m_sb[0:1, 0:wd],
                     m_sb[0:1, 0:wd])
            vvar = dv("tensor_sub", v_sb[0:1, 0:wd], v_sb[0:1, 0:wd],
                      v2_sb[0:1, 0:wd])
            wait("ac", cdv, vvar)
            wait("ac", cpe, st_["lnscr_pe"])
            vln = act(rstd_sb[0:1, 0:wd], v_sb[0:1, 0:wd], AF.Ln,
                      bias=sconsts[0:1, 0:1])
            wait("dv", cac, vln)
            vt = dv("tensor_scalar_mul", rstd_sb[0:1, 0:wd],
                    rstd_sb[0:1, 0:wd], -0.5)
            wait("ac", cdv, vt)
            vex = act(rstd_sb[0:1, 0:wd], rstd_sb[0:1, 0:wd], AF.Exp)
            wait_bank("pe", 2)
            wait_bank("pe", 3)
            wait("pe", cdv, vm2)
            mm(bank(2, wd), sonesr[:, :], m_sb[0:1, 0:wd], True, True)
            wait("pe", cac, vex)
            vbc = mm(bank(3, wd), sonesr[:, :], rstd_sb[0:1, 0:wd], True,
                     True, inc=True)
            st_["lnscr_pe"] = vbc
            return vbc

        def emit_ln_apply(cs, wd, vbc, chunks=None, halo_mask=False,
                          yb=None, ystride=BW):
            yb = ysb if yb is None else yb
            chunks = range(NCH) if chunks is None else chunks
            vap = 0
            for c in chunks:
                xc = xs[:, c * COLS + cs: c * COLS + cs + wd]
                if c == 0:
                    wait("dv", cpe, vbc)
                dv("tensor_sub", scr32[:, 0:wd], xc, bank(2, wd))
                vap = dv("tensor_mul", yb[:, c * ystride: c * ystride + wd],
                         scr32[:, 0:wd], bank(3, wd))
            use_bank(2, "dv", vap)
            use_bank(3, "dv", vap)
            if halo_mask:
                vap = dv("tensor_scalar_mul",
                         yb[:, 0: (NCH - 1) * ystride + 1: ystride],
                         yb[:, 0: (NCH - 1) * ystride + 1: ystride],
                         shm[:, 0:1])
            st_["ln_apply"] = max(st_["ln_apply"], vap)
            return vap

        def emit_ln(cs, wd, halo_mask=False, yb=None):
            v_sum = emit_ln_stats(cs, wd, yb=yb)
            vbc = emit_ln_rstd(cs, wd, v_sum)
            return emit_ln_apply(cs, wd, vbc, halo_mask=halo_mask, yb=yb)

        def emit_ylast(lnwd):
            v = dv("tensor_copy", ylast[:, 0:NCH],
                   ysb[:, lnwd - 1: (NCH - 1) * BW + lnwd: BW])
            st_["ln_apply"] = max(st_["ln_apply"], v)
            return v

        def emit_ffn_block(cs, wd, ln_cs, next_ln=None):
            off = cs - ln_cs
            assert off in (0, 1)
            wait("dv", cpe, st_["ffn_pe"])
            vmix = 0
            for c in range(NCH):
                y0 = c * BW
                yb = ysb[:, y0: y0 + wd + off]
                if off == 1:
                    dv("tensor_sub", xrb[:, y0: y0 + wd], yb[:, 1:1 + wd],
                       yb[:, 0:wd])
                    dv("scalar_tensor_tensor", xkb[:, y0: y0 + wd],
                       xrb[:, y0: y0 + wd], smuk[:, c:c + 1], yb[:, 0:wd],
                       ALU.mult, ALU.add)
                    vmix = dv("scalar_tensor_tensor", xrb[:, y0: y0 + wd],
                              xrb[:, y0: y0 + wd], smur[:, c:c + 1],
                              yb[:, 0:wd], ALU.mult, ALU.add)
                else:
                    dv("tensor_sub", xrb[:, y0: y0 + 1], yb[:, 0:1],
                       ylast[:, c:c + 1])
                    dv("scalar_tensor_tensor", xkb[:, y0: y0 + 1],
                       xrb[:, y0: y0 + 1], smuk[:, c:c + 1],
                       ylast[:, c:c + 1], ALU.mult, ALU.add)
                    vmix = dv("scalar_tensor_tensor", xrb[:, y0: y0 + 1],
                              xrb[:, y0: y0 + 1], smur[:, c:c + 1],
                              ylast[:, c:c + 1], ALU.mult, ALU.add)
                    if wd > 1:
                        dv("tensor_sub", xrb[:, y0 + 1: y0 + wd],
                           yb[:, 1:wd], yb[:, 0:wd - 1])
                        dv("scalar_tensor_tensor", xkb[:, y0 + 1: y0 + wd],
                           xrb[:, y0 + 1: y0 + wd], smuk[:, c:c + 1],
                           yb[:, 0:wd - 1], ALU.mult, ALU.add)
                        vmix = dv("scalar_tensor_tensor",
                                  xrb[:, y0 + 1: y0 + wd],
                                  xrb[:, y0 + 1: y0 + wd], smur[:, c:c + 1],
                                  yb[:, 0:wd - 1], ALU.mult, ALU.add)
            emit_ylast(wd + off)
            n_vsum = None
            if next_ln is not None:
                n_vsum = emit_ln_stats(next_ln[0], next_ln[1])
            wait("pe", cdv, vmix)
            vkct = {}
            for d in range(NCH):
                bi_k = 6 + (d % 2)
                wait_bank("pe", bi_k)
                for c in range(NCH):
                    last = c == NCH - 1
                    vk = mm(bank(bi_k, wd),
                            wk[:, c * E + 128 * d: c * E + 128 * (d + 1)],
                            xkb[:, c * BW: c * BW + wd], c == 0, last,
                            inc=last)
                wait("dv", cpe, vk)
                vrl = dv("tensor_scalar_max", kact[:, d * BW: d * BW + wd],
                         bank(bi_k, wd), 0.0)
                vkct[d] = dv("tensor_mul", kact[:, d * BW: d * BW + wd],
                             kact[:, d * BW: d * BW + wd],
                             kact[:, d * BW: d * BW + wd])
                use_bank(bi_k, "dv", vrl)
            n_vbc = None
            if next_ln is not None:
                n_vbc = emit_ln_rstd(next_ln[0], next_ln[1], n_vsum)
            wait("pe", cdv, vkct[NCH - 1])
            vsig = {}
            vt2d = {}
            vout = 0
            for d in range(NCH):
                par = d % 2
                bi_r, bi_v = 4 + par, 6 + par
                wait_bank("pe", bi_r)
                for c in range(NCH):
                    last = c == NCH - 1
                    vr = mm(bank(bi_r, wd),
                            wr[:, c * E + 128 * d: c * E + 128 * (d + 1)],
                            xrb[:, c * BW: c * BW + wd], c == 0, last,
                            inc=last)
                wait("ac", cpe, vr)
                if d >= 2:
                    wait("ac", cdv, vt2d[d - 2])
                else:
                    wait("ac", cdv, st_["ffn_rscr"])
                vsig[d] = act(rscr[:, par * BW: par * BW + wd],
                              bank(bi_r, wd), AF.Sigmoid)
                use_bank(bi_r, "ac", vsig[d])
                wait_bank("pe", bi_v)
                for c in range(NCH):
                    last = c == NCH - 1
                    vv = mm(bank(bi_v, wd),
                            wv[:, c * E + 128 * d: c * E + 128 * (d + 1)],
                            kact[:, c * BW: c * BW + wd], c == 0, last,
                            inc=last)
                wait("dv", cpe, vv)
                wait("dv", cac, vsig[d])
                vt2d[d] = dv("scalar_tensor_tensor", scr32[:, 0:wd],
                             bank(bi_v, wd), 0.5,
                             rscr[:, par * BW: par * BW + wd],
                             ALU.mult, ALU.mult)
                use_bank(bi_v, "dv", vt2d[d])
                vout = dv("tensor_add",
                          xs[:, d * COLS + cs: d * COLS + cs + wd],
                          xs[:, d * COLS + cs: d * COLS + cs + wd],
                          scr32[:, 0:wd])
                if next_ln is not None:
                    emit_ln_apply(next_ln[0], next_ln[1], n_vbc, chunks=[d])
            st_["ffn_pe"] = vv
            st_["ffn_rscr"] = vt2d[NCH - 1]
            st_["xs_dv"] = vout
            return vout

        wait("pe", s_ld, v_small)
        wait("dv", s_ld, v_small)
        wait("ac", s_ld, v_small)
        wait("pe", s_wffn, v_wffn)
        v_x1 = 0
        LN2B_pre = blocks(2)
        for bi, (cs, wd) in enumerate(FFN1B):
            ln_cs, ln_wd = LN1B[bi]
            if bi == 0:
                emit_ln(ln_cs, ln_wd)
            nxt = LN1B[bi + 1] if bi + 1 < len(FFN1B) else LN2B_pre[0]
            v_x1 = emit_ffn_block(cs, wd, ln_cs, next_ln=nxt)

        if phases < 2:
            emit_all()
            return nc
        groups = [[0, 1, 2, 3], [4, 5, 6, 7]]

        def coll(ins_ap, outs_t):
            v = ccc.bump()
            on("gp", lambda G, i=ins_ap, o=outs_t.ap():
               G.collective_compute("AllGather", ALU.bypass,
                                    replica_groups=groups,
                                    ins=[i], outs=[o]).then_inc(ccc.sem, 1))
            return v

        reg_uid = [0]

        def emit_recopy(dst_ap, src_t, rankblk, blkstride, blkelems):
            uid = reg_uid[0]
            reg_uid[0] += 1

            def f(G, st=src_t, rb=rankblk, bs=blkstride, be=blkelems,
                  dst=dst_ap, uid=uid):
                with G.register(f"ra{uid}") as ra:
                    G.reg_load(ra, srnk[0:1, 0:1])
                    G.reg_mul(ra, ra, bs)
                    G.dma_start(out=dst,
                                in_=bass.AP(st, ra, [[rb, RANKS], [1, be]])
                                ).then_inc(s_cp.sem, 16)
            on("gp", f)
            s_cp.bump()
            return s_cp.v

        v_nrm0 = dv("memset", qkmax[0:1, 0:2], 0.0)
        v_stage_last = 0
        stq = kact
        stq_guard = {i: 0 for i in range(6)}
        STQQ = ["sc", "ac2", "ac2", "sc", "ac2", "sc"]
        vnrm = v_nrm0
        wslot_user = {0: 0, 1: 0}
        for bi, (cs, wd) in enumerate(LN2B):
            if bi == 0:
                vln2 = st_["ln_apply"]
            else:
                vln2 = emit_ln(cs, wd)
            wait("pe", cdv, vln2)
            tok0 = cs - 2
            stq_i = [0]
            for hp in range(8):
                par = hp % 2
                g, hpw = hp // 2, hp % 2
                wait("sy", cpe, wslot_user[par])
                if bi == 0 and hp < 2:
                    wait("sy", s_wqc, v_qc)
                for mi, wsrc in enumerate([qa_bf, ka_bf, va_bf]):
                    slot = qkvw[:, (par * 3 + mi) * NCH * 128:
                                (par * 3 + mi + 1) * NCH * 128]
                    vw = dma("sy", s_wq[par], slot,
                             wsrc[128 * hp:128 * (hp + 1), :])

                def wsl(mi, c, par=par):
                    base = (par * 3 + mi) * NCH * 128 + c * 128
                    return qkvw[:, base: base + 128]

                wait("pe", s_wq[par], vw)
                for mi, bias_t, rows0 in [
                        (0, sbias_q, 512 * g + 128 * hpw),
                        (1, sbias_k, 512 * g + 256 + 128 * hpw)]:
                    bi_ = 4 + mi
                    sslot = stq_i[0] % 6
                    sqq = STQQ[stq_i[0] % 6]
                    sq0 = BW * sslot
                    stq_i[0] += 1
                    wait_bank("pe", bi_)
                    for c in range(NCH):
                        last = c == NCH - 1
                        vq = mm(bank(bi_, wd), wsl(mi, c),
                                ysb[:, c * BW: c * BW + wd], c == 0, last,
                                inc=last)
                    wslot_user[par] = vq
                    st_["ysb_pe"] = vq
                    wait("dv", cpe, vq)
                    wait("dv", s_stqp[sslot], stq_guard[sslot])
                    vev = dv("tensor_scalar_add", stq[:, sq0:sq0 + wd],
                             bank(bi_, wd), bias_t[:, hp:hp + 1])
                    use_bank(bi_, "dv", vev)
                    wait("ac", cdv, vev)
                    wait("ac", cpe, st_["sqt_pe"])
                    vsq = act(sqt[:, 0:wd], stq[:, sq0:sq0 + wd], AF.Square)
                    st_["sq_ac"] = vsq
                    wait("pe", cac, vsq)
                    wait_bank("pe", 1)
                    vn = mm(bank(1, wd, (0, 1)), sonescb[:, :], sqt[:, 0:wd],
                            True, True, inc=True)
                    st_["sqt_pe"] = vn
                    wait("dv", cpe, vn)
                    vrm = dv("reduce_max", nrm_s[0:1, 0:1],
                             bank(1, wd, (0, 1)), axis=AX.X)
                    use_bank(1, "dv", vrm)
                    vnrm = dv("tensor_max", qkmax[0:1, mi:mi + 1],
                              qkmax[0:1, mi:mi + 1], nrm_s[0:1, 0:1])
                    wait("sy" if sqq == "sy" else "ac", cdv, vev)
                    v_stage_last = dma(
                        sqq, s_stqp[sslot],
                        stage_qk_bt[bi].ap()[rows0:rows0 + 128, 0:wd],
                        stq[:, sq0:sq0 + wd])
                    stq_guard[sslot] = v_stage_last
                ntc = wd // 128
                for tc in range(ntc):
                    bi_ = 6 + (tc % 2)
                    sslot = stq_i[0] % 6
                    sqq = STQQ[stq_i[0] % 6]
                    sq0 = BW * sslot
                    stq_i[0] += 1
                    wait_bank("pe", bi_)
                    for c in range(NCH):
                        last = c == NCH - 1
                        vv = mm(bank(bi_, 128),
                                ysb[:, c * BW + tc * 128:
                                    c * BW + (tc + 1) * 128],
                                wsl(2, c), c == 0, last, inc=last)
                    wslot_user[par] = vv
                    st_["ysb_pe"] = vv
                    wait("dv", cpe, vv)
                    wait("dv", s_stqp[sslot], stq_guard[sslot])
                    vvs = dv("scalar_tensor_tensor", stq[:, sq0:sq0 + 128],
                             bank(bi_, 128), 0.125,
                             sbv8[:, 128 * hp: 128 * hp + 128],
                             ALU.mult, ALU.add)
                    use_bank(bi_, "dv", vvs)
                    rows0 = wd * g + tc * 128
                    wait("sy" if sqq == "sy" else "ac", cdv, vvs)
                    v_stage_last = dma(
                        sqq, s_stqp[sslot],
                        stage_v_bt[bi].ap()[rows0:rows0 + 128,
                                            128 * hpw:128 * (hpw + 1)],
                        stq[:, sq0:sq0 + 128])
                    stq_guard[sslot] = v_stage_last
            if bi == NB2 - 1:
                wait("sy", cdv, vnrm)
                v_nst = dma("sy", s_gn, stage_nrm, qkmax[0:1, 0:2])
                wait("gp", s_gn, v_nst)
                v_cn = coll(stage_nrm, g_nrm_t)
            for si in range(6):
                wait("gp", s_stqp[si], s_stqp[si].v)
            coll(stage_qk_bt[bi].ap(), g_qk_bt[bi])
            v_cb = coll(stage_v_bt[bi].ap(), g_v_bt[bi])
            wait("gp", ccc, v_cb)
            t0b = cs - 2
            emit_recopy(
                myqk_bt[bi].ap().rearrange("a b -> (a b)"),
                g_qk_bt[bi], 2048 * wd, 512 * wd, 512 * wd)
            v_myv = emit_recopy(
                myv_bt[bi].ap().rearrange("a b -> (a b)"),
                g_v_bt[bi], RANKS * wd * 256, wd * 256, wd * 256)

        if phases < 3:
            emit_all()
            return nc
        wait("sy", ccc, v_cn)
        v_gn = dma("sy", s_gn, gnrm_sb[0:1, 0:2 * RANKS],
                   bass.AP(g_nrm_t, 0, [[2 * RANKS, 1], [1, 2 * RANKS]]))
        wait("dv", s_gn, v_gn)
        dv("tensor_max", qkg[0:1, 0:2], gnrm_sb[0:1, 0:2], gnrm_sb[0:1, 2:4])
        dv("tensor_max", qkg2[0:1, 0:2], gnrm_sb[0:1, 4:6],
           gnrm_sb[0:1, 6:8])
        dv("tensor_max", qkg[0:1, 0:2], qkg[0:1, 0:2], qkg2[0:1, 0:2])
        vb1 = dv("tensor_mul", nb_s[0:1, 0:1], qkg[0:1, 0:1], qkg[0:1, 1:2])
        wait("ac", cdv, vb1)
        vb2 = act(nb_s[0:1, 0:1], nb_s[0:1, 0:1], AF.Ln)
        wait("dv", cac, vb2)
        vb3 = dv("tensor_scalar", nb_s[0:1, 0:1], nb_s[0:1, 0:1], 0.5, -LN8,
                 ALU.mult, ALU.add)
        wait("ac", cdv, vb3)
        vb4 = act(nb_s[0:1, 0:1], nb_s[0:1, 0:1], AF.Exp)
        wait("dv", cac, vb4)
        vb5 = dv("tensor_scalar_mul", nb_s[0:1, 0:1], nb_s[0:1, 0:1], -1.0)
        wait("pe", cdv, vb5)
        wait_bank("pe", 2)
        vb6 = mm(bank(2, 1), sonesr[:, :], nb_s[0:1, 0:1], True, True,
                 inc=True)
        wait("dv", cpe, vb6)
        vb7 = dv("tensor_copy", negb[:, :], bank(2, 1))
        use_bank(2, "dv", vb7)

        if phases < 4:
            emit_all()
            return nc
        s_user = {0: ("ac", 0), 1: ("ac", 0)}
        pbuf_user = {0: 0, 1: 0}
        kvb_user = {i: 0 for i in range(4)}
        vzbf_prev = {0: 0, 1: 0}
        v_qblk_user = 0
        wait("ac", cdv, vb7)
        wait("sy", s_cp, v_myv)
        wait("ac", s_cp, v_myv)
        v_vones = 0
        for i in range(4):
            vb0 = 4 * KT + 144 * i
            dv("memset", kvbuf[:, vb0 + 64: vb0 + 65], 1.0)
            v_vones = dv("memset", kvbuf[:, vb0 + 72 + 64: vb0 + 72 + 65], 1.0)
        wait("sy", cdv, v_vones)
        first_att = True

        def emit_pv(kt, nkt, diag0, vsc, vexp, vmask, vpv, zpar):
            par = kt % 2
            idx = kt % 4
            vb0 = 4 * KT + 144 * idx
            zA = 2048 + 1024 * zpar
            wait("pe", cac, vexp[kt])
            if kt >= diag0:
                wait("pe", cpl, vmask[kt])
            if kt == 0:
                wait("pe", cdv, vzbf_prev[zpar])
            first, lastk = kt == 0, kt == nkt - 1
            pa = pbuf[:, par * 2 * QB: par * 2 * QB + QB]
            pb = pbuf[:, par * 2 * QB + QB: (par + 1) * 2 * QB]
            mm(PS[0:65, zA:zA + QB], kvbuf[:, vb0: vb0 + 65],
               pa, first, lastk)
            vpv[kt] = mm(PS[0:65, zA + 512:zA + 512 + QB],
                         kvbuf[:, vb0 + 72: vb0 + 72 + 65],
                         pb, first, lastk, inc=True)
            pbuf_user[par] = vpv[kt]
            kvb_user[idx] = max(kvb_user[idx], vpv[kt], vsc[kt])
        att_i = 0
        ATT = [(hpL, g) for hpL in range(2) for g in range(NQB)]
        v_qb = {}
        v_kv = {}
        qb_user = {0: 0, 1: 0}

        def qb_dma(bidx):
            hpL, g = ATT[bidx]
            rq = (g * QB) // TOK
            qc = (g * QB) % TOK
            bq_ = qc // 512
            par = bidx % 2
            wait("sy", cpe, qb_user[par])
            v_qb[bidx] = dma(
                "sy", s_qbp[par], qblk[:, par * QB:(par + 1) * QB],
                myqk_bt[bq_].ap()[512 * rq + 128 * hpL:
                                  512 * rq + 128 * hpL + 128,
                                  qc % 512: qc % 512 + QB])

        def kv_dma(bidx, kt):
            hpL, g = ATT[bidx]
            idx = kt % 4
            rk = (kt * KT) // TOK
            kc = (kt * KT) % TOK
            kb_ = kc // 512
            kcl = kc % 512
            wait("sy", cpe, kvb_user[idx])
            dma("sy", s_kv[idx], kvbuf[:, idx * KT:(idx + 1) * KT],
                myqk_bt[kb_].ap()[512 * rk + 256 + 128 * hpL:
                                  512 * rk + 256 + 128 * hpL + 128,
                                  kcl:kcl + KT])
            vb0 = 4 * KT + 144 * idx
            wdb = LN2B[kb_][1]
            wait("ac", cpe, kvb_user[idx])
            v_kv[(bidx, kt)] = dma(
                "ac2", s_kv[idx],
                bass.AP(kvbuf, vb0,
                        [[4 * KT + 4 * 144, 128], [72, 2], [1, 64]]),
                myv_bt[kb_].ap()[wdb * rk + kcl: wdb * rk + kcl + KT,
                                 128 * hpL:128 * (hpL + 1)]
                .rearrange("t (h d) -> t h d", h=2))

        qb_dma(0)
        kv_dma(0, 0)
        if (0 * QB + QB) // KT > 1:
            kv_dma(0, 1)
        for bidx, (hpL, g) in enumerate(ATT):
            qpar = bidx % 2
            zpar = bidx % 2
            zA = 2048 + 1024 * zpar
            nkt = (g + 1) * QB // KT
            diag0 = g * QB // KT
            vsc = {}
            vpv = {-1: 0}
            vexp = {}
            vmask = {}
            for kt in range(nkt):
                par = kt % 2
                idx = kt % 4
                sbase = 1024 * par
                who, val = s_user[par]
                wait("pe", CTRS[who], val)
                if kt == 0:
                    wait("pe", s_qbp[qpar], v_qb[bidx])
                    if first_att:
                        for b_ in range(8):
                            wait_bank("pe", b_)
                        first_att = False
                wait("pe", s_kv[idx], v_kv[(bidx, kt)])
                mm(PS[:, sbase:sbase + QB],
                   kvbuf[0:64, idx * KT:(idx + 1) * KT],
                   qblk[0:64, qpar * QB:(qpar + 1) * QB], True, True,
                   tp=(0, 0))
                vsc[kt] = mm(PS[:, sbase + QB:sbase + 2 * QB],
                             kvbuf[64:128, idx * KT:(idx + 1) * KT],
                             qblk[64:128, qpar * QB:(qpar + 1) * QB],
                             True, True, tp=(64, 0), inc=True)
                if kt > 0:
                    emit_pv(kt - 1, nkt, diag0, vsc, vexp, vmask, vpv, zpar)
                wait("ac", cpe, vsc[kt])
                vexp[kt] = act(pbuf[:, par * 2 * QB:(par + 1) * 2 * QB],
                               PS[:, sbase:sbase + 2 * QB], AF.Exp,
                               bias=negb[:, 0:1], scale=0.125, chain=False)
                s_user[par] = ("ac", vexp[kt])
                if kt >= diag0:
                    wait("gp", cac, vexp[kt])
                    vmask[kt] = pl(
                        "affine_select",
                        pbuf[:, par * 2 * QB:(par + 1) * 2 * QB]
                        .rearrange("p (h q) -> p h q", h=2),
                        pbuf[:, par * 2 * QB:(par + 1) * 2 * QB]
                        .rearrange("p (h q) -> p h q", h=2),
                        [[0, 2], [1, QB]], ALU.is_ge, 0.0,
                        base=g * QB - kt * KT, channel_multiplier=-1)
                if kt + 2 < nkt:
                    kv_dma(bidx, kt + 2)
                if kt + 3 == nkt and bidx + 1 < len(ATT):
                    qb_dma(bidx + 1)
                    kv_dma(bidx + 1, 0)
                elif kt + 2 == nkt and bidx + 1 < len(ATT):
                    g2 = ATT[bidx + 1][1]
                    if nkt < 3:
                        qb_dma(bidx + 1)
                        kv_dma(bidx + 1, 0)
                    if (g2 + 1) * QB // KT > 1:
                        kv_dma(bidx + 1, 1)

            emit_pv(nkt - 1, nkt, diag0, vsc, vexp, vmask, vpv, zpar)
            qb_user[qpar] = vsc[nkt - 1]
            if nkt == 1 and bidx + 1 < len(ATT):
                qb_dma(bidx + 1)
                kv_dma(bidx + 1, 0)
                g2 = ATT[bidx + 1][1]
                if (g2 + 1) * QB // KT > 1:
                    kv_dma(bidx + 1, 1)
            wait("dv", cpe, vpv[nkt - 1])
            dv("tensor_copy", scale_sb[0:1, 0:QB],
               PS[64:65, zA:zA + QB])
            vcps = dv("tensor_copy", scale_sb[0:1, QB:2 * QB],
                      PS[64:65, zA + 512:zA + 512 + QB])
            wait("pe", cdv, vcps)
            mm(PS[64:128, zA:zA + QB], sonesr[0:1, 0:64],
               scale_sb[0:1, 0:QB], True, True)
            vbc = mm(PS[64:128, zA + 512:zA + 512 + QB], sonesr[0:1, 0:64],
                     scale_sb[0:1, QB:2 * QB], True, True, inc=True)
            wait("dv", cpe, vbc)
            vrc = dv("reciprocal", scale_sb[0:64, 0:QB],
                     PS[64:128, zA:zA + QB])
            vrc2 = dv("reciprocal", scale_sb[0:64, QB:2 * QB],
                      PS[64:128, zA + 512:zA + 512 + QB])
            dv("tensor_mul", zbf[0:64, 0:QB], PS[0:64, zA:zA + QB],
               scale_sb[0:64, 0:QB])
            vz = dv("tensor_mul", zbf[0:64, QB:2 * QB],
                    PS[0:64, zA + 512:zA + 512 + QB],
                    scale_sb[0:64, QB:2 * QB])
            vzbf_prev[zpar] = vz
            use_bank(4 + 2 * zpar, "dv", vz)
            use_bank(5 + 2 * zpar, "dv", vz)
            wait("sy", cdv, vz)
            dma("sy", s_zst,
                z_dram[64 * (2 * hpL): 64 * (2 * hpL) + 64,
                       QB * g: QB * (g + 1)], zbf[0:64, 0:QB])
            v_zstage = dma(
                "sy", s_zst,
                z_dram[64 * (2 * hpL + 1): 64 * (2 * hpL + 1) + 64,
                       QB * g: QB * (g + 1)], zbf[0:64, QB:2 * QB])
            wait("dv", s_zst, v_zstage)
        for par in range(2):
            who, val = s_user[par]
            use_bank(2 * par, who, val)
            use_bank(2 * par + 1, who, val)

        if phases < 5:
            emit_all()
            return nc
        wait("pe", s_wpj, v_wpr)
        zt_sb = xrb
        HB = TOK // 4
        MB = min(256, HB)
        v_x2add = 0
        for hloc in range(4):
            cs = 2 + HB * hloc
            for mb in range(HB // MB):
                wait("sy", cpe, st_["ffn_pe"])
                wait("sy", s_zst, s_zst.v)
                dma("sy", s_gz, zt_sb[0:64, 0:16 * MB],
                    z_dram[64 * hloc:64 * hloc + 64,
                           16 * MB * mb:16 * MB * (mb + 1)])
                vgz = dma("sy", s_gz, zt_sb[64:128, 0:16 * MB],
                          z_dram[64 * hloc:64 * hloc + 64,
                                 16 * MB * mb:16 * MB * (mb + 1)])
                wait("pe", s_gz, vgz)
                for ec in range(NCH):
                    bA = 4 + (ec % 2)
                    bB = 6 + (ec % 2)
                    wait_bank("pe", bA)
                    wait_bank("pe", bB)
                    for c in range(NCH):
                        first, last = c == 0, c == NCH - 1
                        base = c * E + 128 * ec
                        mm(bank(bA, MB), projw[0:64, base:base + 128],
                           zt_sb[0:64, 2 * c: 2 * c + 16 * (MB - 1) + 1: 16],
                           first, last, tp=(0, 0))
                        vp = mm(bank(bB, MB), projw[64:128, base:base + 128],
                                zt_sb[64:128,
                                      2 * c + 1: 2 * c + 16 * (MB - 1) + 2: 16],
                                first, last, tp=(64, 0), inc=last)
                    ccol = cs + MB * mb
                    wait("dv", cpe, vp)
                    dv("tensor_add",
                       xs[:, ec * COLS + ccol: ec * COLS + ccol + MB],
                       xs[:, ec * COLS + ccol: ec * COLS + ccol + MB],
                       bank(bA, MB))
                    v_x2add = dv("tensor_add",
                                 xs[:, ec * COLS + ccol: ec * COLS + ccol + MB],
                                 xs[:, ec * COLS + ccol: ec * COLS + ccol + MB],
                                 bank(bB, MB))
                    use_bank(bA, "dv", v_x2add)
                    use_bank(bB, "dv", v_x2add)
                st_["ffn_pe"] = max(st_["ffn_pe"], vp)
        st_["xs_dv"] = v_x2add

        v_hst = 0
        wait("sy", cdv, v_x2add)
        for c in range(NCH):
            v_hst = dma("sy", s_zst,
                        stage_x2h[c:c + 1, :],
                        xs[:, c * COLS + COLS - 1: c * COLS + COLS])
        wait("gp", s_zst, v_hst)
        v_ch = coll(stage_x2h, g_x2h_t)
        wait("gp", ccc, v_ch)
        uidh = reg_uid[0]
        reg_uid[0] += 1

        def f_h(G, uid=uidh):
            with G.register(f"ra{uid}") as ra:
                G.reg_load(ra, srnk[0:1, 0:1])
                G.reg_add(ra, ra, 3)
                G.reg_alu(ra, ra, 3, ALU.bitwise_and)
                G.reg_mul(ra, ra, 8 * 128)
                G.dma_start(out=myx2h.rearrange("a b -> (a b)"),
                            in_=bass.AP(g_x2h_t, ra, [[1, 8 * 128]])
                            ).then_inc(s_cp.sem, 16)
        on("gp", f_h)
        s_cp.bump()
        v_hcp = s_cp.v
        wait("sy", s_cp, v_hcp)
        v_hld = 0
        for c in range(NCH):
            v_hld = dma("sy", s_gz, xs[:, c * COLS + 1: c * COLS + 2],
                        myx2h[c:c + 1, :])
        v_s3 = emit_ln_stats(*LN3B[1])
        v_b3 = emit_ln_rstd(*LN3B[1], v_s3)
        emit_ln_apply(*LN3B[1], v_b3)
        wait("dv", s_gz, v_hld)
        wait("ac", s_gz, v_hld)
        wait("pe", s_gz, v_hld)
        vs_h = emit_ln_stats(1, 1, yb=ylast, ystride=1)
        vb_h = emit_ln_rstd(1, 1, vs_h)
        emit_ln_apply(1, 1, vb_h, halo_mask=True, yb=ylast, ystride=1)

        v_tr_dv = 0
        v_outst = 0
        for bi in range(1, len(LN3B)):
            cs, wd = LN3B[bi]
            nxt = LN3B[bi + 1] if bi + 1 < len(LN3B) else None
            v_x3 = emit_ffn_block(cs, wd, cs, next_ln=nxt)
            ntc = wd // 128
            for tc in range(ntc):
                c0 = cs + tc * 128
                trow = c0 - 2
                for half in range(2):
                    bi_ = 6 + half
                    wait_bank("pe", bi_)
                    wait("pe", cdv, v_x3)
                    for j in range(4):
                        ecp = 4 * half + j
                        vt = trans(
                            PS[:, 512 * bi_ + 128 * j:
                               512 * bi_ + 128 * j + 128],
                            xs[:, ecp * COLS + c0: ecp * COLS + c0 + 128],
                            inc=(j == 3))
                    wait("dv", cpe, vt)
                    wait("dv", s_out, v_outst)
                    v_tr_dv = dv("tensor_copy", outsb[:, :], bank(bi_, 512))
                    use_bank(bi_, "dv", v_tr_dv)
                    wait("ac", cdv, v_tr_dv)
                    v_outst = dma("sc", s_out,
                                  out[trow:trow + 128,
                                      512 * half:512 * (half + 1)],
                                  outsb[:, :])

        on("ac", lambda A, v=s_out.v: A.wait_ge(s_out.sem, v))

        emit_all()
    return nc


def _noop():
    pass


def _build_emit(block, ops):
        @block.sync
        def _(sync):
            for fn in ops["sy"]:
                fn(sync)

        @block.gpsimd
        def _(gpsimd):
            for fn in ops["gp"]:
                fn(gpsimd)

        @block.tensor
        def _(tensor):
            for fn in ops["pe"]:
                fn(tensor)

        @block.scalar
        def _(scalar):
            for fn in ops["ac"]:
                fn(scalar)

        @block.vector
        def _(vector):
            for fn in ops["dv"]:
                fn(vector)


def make_inputs(x_full, Wq, bq, Wk, bk, Wv, bv, proj, W_r, W_k, W_v, mu_r,
                mu_k, TOK=2048):
    f32 = np.float32
    WqT = np.ascontiguousarray(np.asarray(Wq).reshape(H * DH, E).T.astype(f32))
    WkTa = np.ascontiguousarray(np.asarray(Wk).reshape(H * DH, E).T.astype(f32))
    WvTa = np.ascontiguousarray(np.asarray(Wv).reshape(H * DH, E).T.astype(f32))
    wpr = np.ascontiguousarray(np.asarray(proj).astype(f32))
    wrT = np.ascontiguousarray(np.asarray(W_r).T.astype(f32))
    wkT = np.ascontiguousarray(np.asarray(W_k).T.astype(f32))
    wvT = np.ascontiguousarray(np.asarray(W_v).T.astype(f32))
    bq_t = np.ascontiguousarray(np.asarray(bq).reshape(8, 128).T.astype(f32))
    bk_t = np.ascontiguousarray(np.asarray(bk).reshape(8, 128).T.astype(f32))
    bv_b = np.ascontiguousarray(np.broadcast_to(
        np.asarray(bv).reshape(1, H * DH), (128, H * DH)).astype(f32))
    mur_t = np.ascontiguousarray(np.asarray(mu_r).reshape(8, 128).T.astype(f32))
    muk_t = np.ascontiguousarray(np.asarray(mu_k).reshape(8, 128).T.astype(f32))
    id_ = np.eye(128, dtype=f32)
    onesr_ = np.ones((1, 128), dtype=f32)
    onescb_ = np.ones((128, 1), dtype=ml_dtypes.bfloat16)
    consts_ = np.array([[1e-5, 0.0, 0.0, 0.0]], dtype=f32)

    in_maps = []
    for core in range(8):
        b, i = core // RANKS, core % RANKS
        s0 = i * TOK
        xsl = np.zeros((TOK + 2, E), dtype=f32)
        lo = max(0, s0 - 2)
        xsl[2 - (s0 - lo):] = x_full[b, lo:s0 + TOK]
        xin_ = np.ascontiguousarray(xsl.T)
        hm = np.full((128, 1), 0.0 if i == 0 else 1.0, dtype=f32)
        in_maps.append(dict(
            xin=xin_, wrT=wrT, wkT=wkT, wvT=wvT, wqaT=WqT, wkaT=WkTa,
            wvaT=WvTa, wpr=wpr, bq_t=bq_t, bk_t=bk_t, bv_bc=bv_b,
            mur_t=mur_t, muk_t=muk_t, hmask=hm, id128=id_, onesr=onesr_,
            onescb=onescb_, consts=consts_,
            rnk=np.array([[i]], dtype=np.int32)))
    return in_maps


def kernel(L0_attn_x, L0_attn_Wq, L0_attn_bq, L0_attn_Wk, L0_attn_bk,
           L0_attn_Wv, L0_attn_bv, L0_attn_proj, L0_ffn_W_r, L0_ffn_W_k,
           L0_ffn_W_v, L0_ffn_mu_r, L0_ffn_mu_k):
    TOK = 2048
    x = np.asarray(L0_attn_x, dtype=np.float32)
    in_maps = make_inputs(
        x, L0_attn_Wq, L0_attn_bq, L0_attn_Wk, L0_attn_bk, L0_attn_Wv,
        L0_attn_bv, L0_attn_proj, L0_ffn_W_r, L0_ffn_W_k, L0_ffn_W_v,
        L0_ffn_mu_r, L0_ffn_mu_k, TOK=TOK)
    nc = build_nc(TOK=TOK, QB=512)
    res = run_bass_kernel_spmd(nc, in_maps, core_ids=list(range(8)))
    outf = np.empty((2, RANKS * TOK, E), dtype=np.float32)
    for core in range(8):
        b, i = core // RANKS, core % RANKS
        outf[b, i * TOK:(i + 1) * TOK] = res.results[core]["out"]
    return outf


# version 21
# speedup vs baseline: 1.0212x; 1.0212x over previous
import numpy as np
import ml_dtypes
from contextlib import ExitStack

import concourse.bass as bass
import concourse.mybir as mybir
from concourse.bass_utils import run_bass_kernel_spmd

F32 = mybir.dt.float32
BF16 = mybir.dt.bfloat16
I32 = mybir.dt.int32
AF = mybir.ActivationFunctionType
ALU = mybir.AluOpType
AX = mybir.AxisListType

E = 1024
NCH = 8
H = 16
DH = 64
RANKS = 4
BW = 512
KT = 128
LN8 = 2.0794415416798357


def build_nc(TOK=2048, QB=512, phases=9, att_n=10**9):
    S = RANKS * TOK
    COLS = TOK + 2
    NQB = S // QB
    QBW = TOK // QB
    assert TOK % QB == 0 and QB % KT == 0 and 128 <= QB <= 512

    nc = bass.Bass("TRN2", target_bir_lowering=False)

    xin = nc.dram_tensor("xin", [E, COLS], F32, kind="ExternalInput").ap()
    wrT = nc.dram_tensor("wrT", [E, E], F32, kind="ExternalInput").ap()
    wkT = nc.dram_tensor("wkT", [E, E], F32, kind="ExternalInput").ap()
    wvT = nc.dram_tensor("wvT", [E, E], F32, kind="ExternalInput").ap()
    wqaT = nc.dram_tensor("wqaT", [E, E], F32, kind="ExternalInput").ap()
    wkaT = nc.dram_tensor("wkaT", [E, E], F32, kind="ExternalInput").ap()
    wvaT = nc.dram_tensor("wvaT", [E, E], F32, kind="ExternalInput").ap()
    wpr = nc.dram_tensor("wpr", [E, E], F32, kind="ExternalInput").ap()
    bq_t = nc.dram_tensor("bq_t", [128, 8], F32, kind="ExternalInput").ap()
    bk_t = nc.dram_tensor("bk_t", [128, 8], F32, kind="ExternalInput").ap()
    bv_bc = nc.dram_tensor("bv_bc", [128, E], F32, kind="ExternalInput").ap()
    mur_t = nc.dram_tensor("mur_t", [128, 8], F32, kind="ExternalInput").ap()
    muk_t = nc.dram_tensor("muk_t", [128, 8], F32, kind="ExternalInput").ap()
    hmask = nc.dram_tensor("hmask", [128, 1], F32, kind="ExternalInput").ap()
    id128 = nc.dram_tensor("id128", [128, 128], F32, kind="ExternalInput").ap()
    onesr = nc.dram_tensor("onesr", [1, 128], F32, kind="ExternalInput").ap()
    onescb = nc.dram_tensor("onescb", [128, 1], BF16, kind="ExternalInput").ap()
    consts = nc.dram_tensor("consts", [1, 4], F32, kind="ExternalInput").ap()
    rnk = nc.dram_tensor("rnk", [1, 1], I32, kind="ExternalInput").ap()
    out = nc.dram_tensor("out", [TOK, E], F32, kind="ExternalOutput").ap()

    qa_bf = nc.dram_tensor("qa_bf", [E, E], BF16).ap()
    ka_bf = nc.dram_tensor("ka_bf", [E, E], BF16).ap()
    va_bf = nc.dram_tensor("va_bf", [E, E], BF16).ap()


    z_dram = nc.dram_tensor("z_dram", [256, S], BF16).ap()
    stage_x2h = nc.dram_tensor("stage_x2h", [8, 128], F32).ap()
    g_x2h_t = nc.dram_tensor("g_x2h", [RANKS * 8, 128], F32)
    myx2h = nc.dram_tensor("myx2h", [8, 128], F32).ap()
    stage_nrm = nc.dram_tensor("stage_nrm", [1, 2], F32).ap()
    g_nrm_t = nc.dram_tensor("g_nrm", [RANKS, 2], F32)


    def blocks(lo):
        out_ = []
        c = lo
        while c < COLS:
            if c < 2:
                w = 2 - c
            else:
                w = min(BW - ((c - 2) % BW), COLS - c)
            out_.append((c, w))
            c += w
        return out_

    LN1B = blocks(0)
    FFN1B = blocks(1)
    LN2B = blocks(2)
    LN3B = blocks(1)
    stage_qk_bt = [nc.dram_tensor(f"stage_qk{b}", [RANKS * 512, w], BF16)
                   for b, (c, w) in enumerate(LN2B)]
    g_qk_bt = [nc.dram_tensor(f"g_qk{b}", [RANKS * RANKS * 512, w], BF16)
               for b, (c, w) in enumerate(LN2B)]
    stage_v_bt = [nc.dram_tensor(f"stage_v{b}", [RANKS * w, 256], BF16)
                  for b, (c, w) in enumerate(LN2B)]
    g_v_bt = [nc.dram_tensor(f"g_v{b}", [RANKS * RANKS * w, 256], BF16)
              for b, (c, w) in enumerate(LN2B)]
    myqk_bt = [nc.dram_tensor(f"myqk{b}", [RANKS * 512, w], BF16)
               for b, (c, w) in enumerate(LN2B)]
    myv_bt = [nc.dram_tensor(f"myv{b}", [RANKS * w, 256], BF16)
              for b, (c, w) in enumerate(LN2B)]
    NB2 = len(LN2B)

    with ExitStack() as ctx:
        en = ctx.enter_context
        xs = en(nc.sbuf_tensor("xs", [128, NCH * COLS], F32))
        ysb = en(nc.sbuf_tensor("ysb", [128, NCH * BW], BF16))
        ylast = en(nc.sbuf_tensor("ylast", [128, NCH], BF16))
        scr32 = en(nc.sbuf_tensor("scr32", [128, BW], F32))
        xrb = en(nc.sbuf_tensor("xrb", [128, NCH * BW], BF16))
        xkb = en(nc.sbuf_tensor("xkb", [128, NCH * BW], BF16))
        rscr = en(nc.sbuf_tensor("rscr", [128, 2 * BW], BF16))
        kact = en(nc.sbuf_tensor("kact", [128, NCH * BW], BF16))
        wr = en(nc.sbuf_tensor("wr", [128, NCH * E], BF16))
        wk = en(nc.sbuf_tensor("wk", [128, NCH * E], BF16))
        wv = en(nc.sbuf_tensor("wv", [128, NCH * E], BF16))
        qkvw = en(nc.sbuf_tensor("qkvw", [128, 2 * 3 * NCH * 128], BF16))
        projw = en(nc.sbuf_tensor("projw", [128, NCH * E], BF16))
        qblk = en(nc.sbuf_tensor("qblk", [128, 2 * QB], BF16))
        kvbuf = en(nc.sbuf_tensor("kvbuf", [128, 4 * KT + 4 * 144], BF16))
        pbuf = en(nc.sbuf_tensor("pbuf", [128, 2 * 2 * QB], BF16))
        zbf = en(nc.sbuf_tensor("zbf", [64, 2 * QB], BF16))
        scale_sb = en(nc.sbuf_tensor("scale_sb", [64, 2 * QB], F32))
        sqt = en(nc.sbuf_tensor("sqt", [128, BW], BF16))
        outsb = en(nc.sbuf_tensor("outsb", [128, 512], F32))
        sbias_q = en(nc.sbuf_tensor("sbias_q", [128, 8], F32))
        sbias_k = en(nc.sbuf_tensor("sbias_k", [128, 8], F32))
        sbv8 = en(nc.sbuf_tensor("sbv8", [128, E], BF16))
        smur = en(nc.sbuf_tensor("smur", [128, 8], F32))
        smuk = en(nc.sbuf_tensor("smuk", [128, 8], F32))
        shm = en(nc.sbuf_tensor("shm", [128, 1], F32))
        sid = en(nc.sbuf_tensor("sid", [128, 128], F32))
        sonesr = en(nc.sbuf_tensor("sonesr", [1, 128], F32))
        sonescb = en(nc.sbuf_tensor("sonescb", [128, 1], BF16))
        sconsts = en(nc.sbuf_tensor("sconsts", [1, 4], F32))
        srnk = en(nc.sbuf_tensor("srnk", [1, 1], I32))
        m_sb = en(nc.sbuf_tensor("m_sb", [1, BW], F32))
        v_sb = en(nc.sbuf_tensor("v_sb", [1, BW], F32))
        v2_sb = en(nc.sbuf_tensor("v2_sb", [1, BW], F32))
        rstd_sb = en(nc.sbuf_tensor("rstd_sb", [1, BW], F32))
        qkmax = en(nc.sbuf_tensor("qkmax", [1, 2], F32))
        nrm_s = en(nc.sbuf_tensor("nrm_s", [1, 1], F32))
        gnrm_sb = en(nc.sbuf_tensor("gnrm_sb", [1, RANKS * 2], F32))
        qkg = en(nc.sbuf_tensor("qkg", [1, 2], F32))
        qkg2 = en(nc.sbuf_tensor("qkg2", [1, 2], F32))
        negb = en(nc.sbuf_tensor("negb", [128, 1], F32))
        nb_s = en(nc.sbuf_tensor("nb_s", [1, 1], F32))

        PS = en(nc.psum_tensor("PS", [128, 4096], F32))

        class Ctr:
            def __init__(self, name, step=1):
                self.sem = en(nc.semaphore(name))
                self.step = step
                self.v = 0

            def bump(self):
                self.v += self.step
                return self.v

        s_ld = Ctr("s_ld", 16)
        s_wffn = Ctr("s_wffn", 16)
        s_wqc = Ctr("s_wqc", 16)
        s_wpj = Ctr("s_wpj", 16)
        s_wq = [Ctr("s_wq0", 16), Ctr("s_wq1", 16)]
        s_stqp = [Ctr(f"s_stq{i}", 16) for i in range(6)]
        s_qbp = [Ctr("s_qb0", 16), Ctr("s_qb1", 16)]
        s_kv = [Ctr(f"s_kv{i}", 16) for i in range(4)]
        s_zst = Ctr("s_zst", 16)
        s_gn = Ctr("s_gn", 16)
        s_gz = Ctr("s_gz", 16)
        s_out = Ctr("s_out", 16)
        s_cp = Ctr("s_cp", 16)
        s_bv = Ctr("s_bv", 16)
        cpe = Ctr("cpe")
        cac = Ctr("cac")
        cdv = Ctr("cdv")
        cpl = Ctr("cpl")
        ccc = Ctr("ccc")
        CTRS = {"pe": cpe, "ac": cac, "dv": cdv, "pl": cpl}

        block = en(nc.Block())
        ops = {"sy": [], "gp": [], "pe": [], "ac": [], "dv": []}

        def emit_all():
            _build_emit(block, ops)

        def on(q, fn):
            ops[q].append(fn)

        def wait(q, ctr, val):
            if val and val > 0:
                on(q, lambda EG, s=ctr.sem, v=val: EG.wait_ge(s, v))

        def dma(q, c, out_ap, in_ap, **kw):
            eq = {"sy": "sy", "gp": "gp", "sc": "ac", "ac2": "ac",
                  "vd": "dv"}[q]
            c.bump()
            on(eq, lambda EG, o=out_ap, i=in_ap, s=c.sem, kw=kw:
               EG.dma_start(out=o, in_=i, **kw).then_inc(s, 16))
            return c.v

        def mm(out_ap, lhsT, rhs, start, stop, tp=None, inc=False):
            if inc:
                cpe.bump()
            v = cpe.v

            def f(T, o=out_ap, l=lhsT, r=rhs, st=start, sp=stop, tp=tp,
                  inc=inc):
                i = T.matmul(o, l, r, start=st, stop=sp, tile_position=tp)
                if inc:
                    i.then_inc(cpe.sem, 1)
            on("pe", f)
            return v

        def trans(out_ap, in_ap, inc=False):
            if inc:
                cpe.bump()
            v = cpe.v

            def f(T, o=out_ap, i=in_ap, inc=inc):
                r = T.transpose(o, i, sid[:, :])
                if inc:
                    r.then_inc(cpe.sem, 1)
            on("pe", f)
            return v

        def act(out_ap, in_ap, func, bias=0.0, scale=1.0, chain=True):
            if chain:
                wait("ac", cac, cac.v)
            v = cac.bump()

            def f(A, o=out_ap, i=in_ap, fn=func, b=bias, s=scale):
                A.activation(o, i, fn, bias=b, scale=s).then_inc(cac.sem, 1)
            on("ac", f)
            return v

        def dv(name, *args, chain=True, **kw):
            if chain:
                wait("dv", cdv, cdv.v)
            v = cdv.bump()

            def f(V, n=name, a=args, kw=kw):
                getattr(V, n)(*a, **kw).then_inc(cdv.sem, 1)
            on("dv", f)
            return v

        def pl(name, *args, **kw):
            wait("gp", cpl, cpl.v)
            v = cpl.bump()

            def f(G, n=name, a=args, kw=kw):
                getattr(G, n)(*a, **kw).then_inc(cpl.sem, 1)
            on("gp", f)
            return v

        def bank(i, w=512, rows=(0, 128)):
            return PS[rows[0]:rows[1], 512 * i: 512 * i + w]

        last_bank_use = {i: ("dv", 0) for i in range(8)}

        def use_bank(i, who, val):
            last_bank_use[i] = (who, val)

        def wait_bank(q, i):
            who, val = last_bank_use[i]
            wait(q, CTRS[who], val)

        v_x = dma("sy", s_ld, xs[:, :].rearrange("p (c t) -> p c t", c=NCH),
                  xin.rearrange("(c p) t -> p c t", p=128))
        for dst, src in [(sbias_q[:, :], bq_t), (sbias_k[:, :], bk_t),
                         (smur[:, :], mur_t),
                         (smuk[:, :], muk_t), (shm[:, :], hmask),
                         (sid[:, :], id128), (sonesr[:, :], onesr),
                         (sonescb[:, :], onescb), (sconsts[:, :], consts),
                         (srnk[:, :], rnk)]:
            v_small = dma("sy", s_ld, dst, src)
        v_wffn = 0
        for dst, src in [(wr, wrT), (wk, wkT), (wv, wvT)]:
            v_wffn = dma("gp", s_wffn,
                         dst[:, :].rearrange("p (c d) -> p c d", c=NCH),
                         src.rearrange("(c p) d -> p c d", p=128))
        v_qc = 0
        for dst, src in [(qa_bf, wqaT), (ka_bf, wkaT), (va_bf, wvaT)]:
            for hh in range(8):
                v_qc = dma("gp", s_wqc,
                           dst[hh * 128:(hh + 1) * 128, :]
                           .rearrange("p (c d) -> p c d", d=128),
                           src[:, 128 * hh:128 * (hh + 1)]
                           .rearrange("(c p) d -> p c d", p=128))
        v_wpr = dma("gp", s_wpj,
                    projw[:, :].rearrange("p (c d) -> p c d", c=NCH),
                    wpr.rearrange("(c p) d -> p c d", p=128))
        v_bv = dma("gp", s_bv, sbv8[:, :], bv_bc)
        wait("dv", s_ld, v_small)
        wait("dv", s_bv, v_bv)
        v_bv8 = dv("tensor_scalar_mul", sbv8[:, :], sbv8[:, :], 0.125)

        st_ = {"ln_apply": 0,
               "ysb_pe": 0,
               "ffn_pe": 0,
               "ffn_rscr": 0,
               "sq_ac": 0,
               "kact_ac": 0,
               "sqt_pe": 0,
               "lnscr_pe": 0,
               "xs_dv": 0}

        def emit_ln_stats(cs, wd, yb=None, ystride=BW):
            yb = ysb if yb is None else yb
            v_sum = 0
            for c in range(NCH):
                xc = xs[:, c * COLS + cs: c * COLS + cs + wd]
                if c == 0:
                    wait("dv", cdv, st_["ln_apply"])
                    wait("dv", cpe, st_["ysb_pe"])
                    wait("ac", cdv, st_["xs_dv"])
                vcp = dv("tensor_copy", yb[:, c * ystride: c * ystride + wd],
                         xc)
                wait("ac", cpe, st_["sqt_pe"])
                vsq = act(sqt[:, 0:wd], xc, AF.Square)
                if c == 0:
                    wait_bank("pe", 0)
                    wait_bank("pe", 1)
                wait("pe", cdv, vcp)
                mm(bank(0, wd, (0, 1)), sonescb[:, :],
                   yb[:, c * ystride: c * ystride + wd], c == 0,
                   c == NCH - 1)
                wait("pe", cac, vsq)
                v_sum = mm(bank(1, wd, (0, 1)), sonescb[:, :], sqt[:, 0:wd],
                           c == 0, c == NCH - 1, inc=True)
                st_["sqt_pe"] = v_sum
            st_["sq_ac"] = vsq
            st_["ysb_pe"] = v_sum
            return v_sum

        def emit_ln_rstd(cs, wd, v_sum):
            wait("dv", cpe, max(v_sum, st_["lnscr_pe"]))
            vm = dv("tensor_scalar_mul", m_sb[0:1, 0:wd],
                    bank(0, wd, (0, 1)), 1.0 / E)
            ve1 = dv("tensor_scalar_mul", v_sb[0:1, 0:wd],
                     bank(1, wd, (0, 1)), 1.0 / E)
            use_bank(0, "dv", vm)
            use_bank(1, "dv", ve1)
            vm2 = dv("tensor_mul", v2_sb[0:1, 0:wd], m_sb[0:1, 0:wd],
                     m_sb[0:1, 0:wd])
            vvar = dv("tensor_sub", v_sb[0:1, 0:wd], v_sb[0:1, 0:wd],
                      v2_sb[0:1, 0:wd])
            wait("ac", cdv, vvar)
            wait("ac", cpe, st_["lnscr_pe"])
            vln = act(rstd_sb[0:1, 0:wd], v_sb[0:1, 0:wd], AF.Ln,
                      bias=sconsts[0:1, 0:1])
            wait("dv", cac, vln)
            vt = dv("tensor_scalar_mul", rstd_sb[0:1, 0:wd],
                    rstd_sb[0:1, 0:wd], -0.5)
            wait("ac", cdv, vt)
            vex = act(rstd_sb[0:1, 0:wd], rstd_sb[0:1, 0:wd], AF.Exp)
            wait_bank("pe", 2)
            wait_bank("pe", 3)
            wait("pe", cdv, vm2)
            mm(bank(2, wd), sonesr[:, :], m_sb[0:1, 0:wd], True, True)
            wait("pe", cac, vex)
            vbc = mm(bank(3, wd), sonesr[:, :], rstd_sb[0:1, 0:wd], True,
                     True, inc=True)
            st_["lnscr_pe"] = vbc
            return vbc

        def emit_ln_apply(cs, wd, vbc, chunks=None, halo_mask=False,
                          yb=None, ystride=BW):
            yb = ysb if yb is None else yb
            chunks = range(NCH) if chunks is None else chunks
            vap = 0
            for c in chunks:
                xc = xs[:, c * COLS + cs: c * COLS + cs + wd]
                if c == 0:
                    wait("dv", cpe, vbc)
                dv("tensor_sub", scr32[:, 0:wd], xc, bank(2, wd))
                vap = dv("tensor_mul", yb[:, c * ystride: c * ystride + wd],
                         scr32[:, 0:wd], bank(3, wd))
            use_bank(2, "dv", vap)
            use_bank(3, "dv", vap)
            if halo_mask:
                vap = dv("tensor_scalar_mul",
                         yb[:, 0: (NCH - 1) * ystride + 1: ystride],
                         yb[:, 0: (NCH - 1) * ystride + 1: ystride],
                         shm[:, 0:1])
            st_["ln_apply"] = max(st_["ln_apply"], vap)
            return vap

        def emit_ln(cs, wd, halo_mask=False, yb=None):
            v_sum = emit_ln_stats(cs, wd, yb=yb)
            vbc = emit_ln_rstd(cs, wd, v_sum)
            return emit_ln_apply(cs, wd, vbc, halo_mask=halo_mask, yb=yb)

        def emit_ylast(lnwd):
            v = dv("tensor_copy", ylast[:, 0:NCH],
                   ysb[:, lnwd - 1: (NCH - 1) * BW + lnwd: BW])
            st_["ln_apply"] = max(st_["ln_apply"], v)
            return v

        def emit_ffn_block(cs, wd, ln_cs, next_ln=None):
            off = cs - ln_cs
            assert off in (0, 1)
            wait("dv", cpe, st_["ffn_pe"])
            vmix = 0
            for c in range(NCH):
                y0 = c * BW
                yb = ysb[:, y0: y0 + wd + off]
                if off == 1:
                    dv("tensor_sub", xrb[:, y0: y0 + wd], yb[:, 1:1 + wd],
                       yb[:, 0:wd])
                    dv("scalar_tensor_tensor", xkb[:, y0: y0 + wd],
                       xrb[:, y0: y0 + wd], smuk[:, c:c + 1], yb[:, 0:wd],
                       ALU.mult, ALU.add)
                    vmix = dv("scalar_tensor_tensor", xrb[:, y0: y0 + wd],
                              xrb[:, y0: y0 + wd], smur[:, c:c + 1],
                              yb[:, 0:wd], ALU.mult, ALU.add)
                else:
                    dv("tensor_sub", xrb[:, y0: y0 + 1], yb[:, 0:1],
                       ylast[:, c:c + 1])
                    dv("scalar_tensor_tensor", xkb[:, y0: y0 + 1],
                       xrb[:, y0: y0 + 1], smuk[:, c:c + 1],
                       ylast[:, c:c + 1], ALU.mult, ALU.add)
                    vmix = dv("scalar_tensor_tensor", xrb[:, y0: y0 + 1],
                              xrb[:, y0: y0 + 1], smur[:, c:c + 1],
                              ylast[:, c:c + 1], ALU.mult, ALU.add)
                    if wd > 1:
                        dv("tensor_sub", xrb[:, y0 + 1: y0 + wd],
                           yb[:, 1:wd], yb[:, 0:wd - 1])
                        dv("scalar_tensor_tensor", xkb[:, y0 + 1: y0 + wd],
                           xrb[:, y0 + 1: y0 + wd], smuk[:, c:c + 1],
                           yb[:, 0:wd - 1], ALU.mult, ALU.add)
                        vmix = dv("scalar_tensor_tensor",
                                  xrb[:, y0 + 1: y0 + wd],
                                  xrb[:, y0 + 1: y0 + wd], smur[:, c:c + 1],
                                  yb[:, 0:wd - 1], ALU.mult, ALU.add)
            emit_ylast(wd + off)
            n_vsum = None
            if next_ln is not None:
                n_vsum = emit_ln_stats(next_ln[0], next_ln[1])
            wait("pe", cdv, vmix)
            vkct = {}
            for d in range(NCH):
                bi_k = 6 + (d % 2)
                wait_bank("pe", bi_k)
                for c in range(NCH):
                    last = c == NCH - 1
                    vk = mm(bank(bi_k, wd),
                            wk[:, c * E + 128 * d: c * E + 128 * (d + 1)],
                            xkb[:, c * BW: c * BW + wd], c == 0, last,
                            inc=last)
                wait("dv", cpe, vk)
                if d == 0:
                    wait("dv", cac, st_["kact_ac"])
                vrl = dv("tensor_scalar_max", kact[:, d * BW: d * BW + wd],
                         bank(bi_k, wd), 0.0)
                wait("ac", cdv, vrl)
                vkct[d] = act(kact[:, d * BW: d * BW + wd],
                              kact[:, d * BW: d * BW + wd], AF.Square)
                use_bank(bi_k, "dv", vrl)
            n_vbc = None
            if next_ln is not None:
                n_vbc = emit_ln_rstd(next_ln[0], next_ln[1], n_vsum)
            st_["kact_ac"] = vkct[NCH - 1]
            wait("pe", cac, vkct[NCH - 1])
            vsig = {}
            vt2d = {}
            vout = 0
            for d in range(NCH):
                par = d % 2
                bi_r, bi_v = 4 + par, 6 + par
                wait_bank("pe", bi_r)
                for c in range(NCH):
                    last = c == NCH - 1
                    vr = mm(bank(bi_r, wd),
                            wr[:, c * E + 128 * d: c * E + 128 * (d + 1)],
                            xrb[:, c * BW: c * BW + wd], c == 0, last,
                            inc=last)
                wait("ac", cpe, vr)
                if d >= 2:
                    wait("ac", cdv, vt2d[d - 2])
                else:
                    wait("ac", cdv, st_["ffn_rscr"])
                vsig[d] = act(rscr[:, par * BW: par * BW + wd],
                              bank(bi_r, wd), AF.Sigmoid)
                use_bank(bi_r, "ac", vsig[d])
                wait_bank("pe", bi_v)
                for c in range(NCH):
                    last = c == NCH - 1
                    vv = mm(bank(bi_v, wd),
                            wv[:, c * E + 128 * d: c * E + 128 * (d + 1)],
                            kact[:, c * BW: c * BW + wd], c == 0, last,
                            inc=last)
                wait("dv", cpe, vv)
                wait("dv", cac, vsig[d])
                vt2d[d] = dv("scalar_tensor_tensor", scr32[:, 0:wd],
                             bank(bi_v, wd), 0.5,
                             rscr[:, par * BW: par * BW + wd],
                             ALU.mult, ALU.mult)
                use_bank(bi_v, "dv", vt2d[d])
                vout = dv("tensor_add",
                          xs[:, d * COLS + cs: d * COLS + cs + wd],
                          xs[:, d * COLS + cs: d * COLS + cs + wd],
                          scr32[:, 0:wd])
                if next_ln is not None:
                    emit_ln_apply(next_ln[0], next_ln[1], n_vbc, chunks=[d])
            st_["ffn_pe"] = vv
            st_["ffn_rscr"] = vt2d[NCH - 1]
            st_["xs_dv"] = vout
            return vout

        wait("pe", s_ld, v_small)
        wait("dv", s_ld, v_small)
        wait("ac", s_ld, v_small)
        wait("pe", s_wffn, v_wffn)
        v_x1 = 0
        LN2B_pre = blocks(2)
        for bi, (cs, wd) in enumerate(FFN1B):
            ln_cs, ln_wd = LN1B[bi]
            if bi == 0:
                emit_ln(ln_cs, ln_wd)
            nxt = LN1B[bi + 1] if bi + 1 < len(FFN1B) else LN2B_pre[0]
            v_x1 = emit_ffn_block(cs, wd, ln_cs, next_ln=nxt)

        if phases < 2:
            emit_all()
            return nc
        groups = [[0, 1, 2, 3], [4, 5, 6, 7]]

        def coll(ins_ap, outs_t):
            v = ccc.bump()
            on("gp", lambda G, i=ins_ap, o=outs_t.ap():
               G.collective_compute("AllGather", ALU.bypass,
                                    replica_groups=groups,
                                    ins=[i], outs=[o]).then_inc(ccc.sem, 1))
            return v

        reg_uid = [0]

        def emit_recopy(dst_ap, src_t, rankblk, blkstride, blkelems):
            uid = reg_uid[0]
            reg_uid[0] += 1

            def f(G, st=src_t, rb=rankblk, bs=blkstride, be=blkelems,
                  dst=dst_ap, uid=uid):
                with G.register(f"ra{uid}") as ra:
                    G.reg_load(ra, srnk[0:1, 0:1])
                    G.reg_mul(ra, ra, bs)
                    G.dma_start(out=dst,
                                in_=bass.AP(st, ra, [[rb, RANKS], [1, be]])
                                ).then_inc(s_cp.sem, 16)
            on("gp", f)
            s_cp.bump()
            return s_cp.v

        v_nrm0 = dv("memset", qkmax[0:1, 0:2], 0.0)
        v_stage_last = 0
        stq = kact
        stq_guard = {i: 0 for i in range(6)}
        STQQ = ["sc", "ac2", "ac2", "sc", "ac2", "sc"]
        vnrm = v_nrm0
        wslot_user = {0: 0, 1: 0}
        for bi, (cs, wd) in enumerate(LN2B):
            if bi == 0:
                vln2 = st_["ln_apply"]
            else:
                vln2 = emit_ln(cs, wd)
            wait("pe", cdv, vln2)
            tok0 = cs - 2
            stq_i = [0]
            for hp in range(8):
                par = hp % 2
                g, hpw = hp // 2, hp % 2
                wait("sy", cpe, wslot_user[par])
                if bi == 0 and hp < 2:
                    wait("sy", s_wqc, v_qc)
                for mi, wsrc in enumerate([qa_bf, ka_bf, va_bf]):
                    slot = qkvw[:, (par * 3 + mi) * NCH * 128:
                                (par * 3 + mi + 1) * NCH * 128]
                    vw = dma("sy", s_wq[par], slot,
                             wsrc[128 * hp:128 * (hp + 1), :])

                def wsl(mi, c, par=par):
                    base = (par * 3 + mi) * NCH * 128 + c * 128
                    return qkvw[:, base: base + 128]

                wait("pe", s_wq[par], vw)
                for mi, bias_t, rows0 in [
                        (0, sbias_q, 512 * g + 128 * hpw),
                        (1, sbias_k, 512 * g + 256 + 128 * hpw)]:
                    bi_ = 4 + mi
                    sslot = stq_i[0] % 6
                    sqq = STQQ[stq_i[0] % 6]
                    sq0 = BW * sslot
                    stq_i[0] += 1
                    wait_bank("pe", bi_)
                    for c in range(NCH):
                        last = c == NCH - 1
                        vq = mm(bank(bi_, wd), wsl(mi, c),
                                ysb[:, c * BW: c * BW + wd], c == 0, last,
                                inc=last)
                    wslot_user[par] = vq
                    st_["ysb_pe"] = vq
                    wait("dv", cpe, vq)
                    wait("dv", s_stqp[sslot], stq_guard[sslot])
                    vev = dv("tensor_scalar_add", stq[:, sq0:sq0 + wd],
                             bank(bi_, wd), bias_t[:, hp:hp + 1])
                    use_bank(bi_, "dv", vev)
                    wait("ac", cdv, vev)
                    wait("ac", cpe, st_["sqt_pe"])
                    vsq = act(sqt[:, 0:wd], stq[:, sq0:sq0 + wd], AF.Square)
                    st_["sq_ac"] = vsq
                    wait("pe", cac, vsq)
                    wait_bank("pe", 1)
                    vn = mm(bank(1, wd, (0, 1)), sonescb[:, :], sqt[:, 0:wd],
                            True, True, inc=True)
                    st_["sqt_pe"] = vn
                    wait("dv", cpe, vn)
                    vrm = dv("reduce_max", nrm_s[0:1, 0:1],
                             bank(1, wd, (0, 1)), axis=AX.X)
                    use_bank(1, "dv", vrm)
                    vnrm = dv("tensor_max", qkmax[0:1, mi:mi + 1],
                              qkmax[0:1, mi:mi + 1], nrm_s[0:1, 0:1])
                    wait("sy" if sqq == "sy" else "ac", cdv, vev)
                    v_stage_last = dma(
                        sqq, s_stqp[sslot],
                        stage_qk_bt[bi].ap()[rows0:rows0 + 128, 0:wd],
                        stq[:, sq0:sq0 + wd])
                    stq_guard[sslot] = v_stage_last
                ntc = wd // 128
                for tc in range(ntc):
                    bi_ = 6 + (tc % 2)
                    sslot = stq_i[0] % 6
                    sqq = STQQ[stq_i[0] % 6]
                    sq0 = BW * sslot
                    stq_i[0] += 1
                    wait_bank("pe", bi_)
                    for c in range(NCH):
                        last = c == NCH - 1
                        vv = mm(bank(bi_, 128),
                                ysb[:, c * BW + tc * 128:
                                    c * BW + (tc + 1) * 128],
                                wsl(2, c), c == 0, last, inc=last)
                    wslot_user[par] = vv
                    st_["ysb_pe"] = vv
                    wait("dv", cpe, vv)
                    wait("dv", s_stqp[sslot], stq_guard[sslot])
                    vvs = dv("scalar_tensor_tensor", stq[:, sq0:sq0 + 128],
                             bank(bi_, 128), 0.125,
                             sbv8[:, 128 * hp: 128 * hp + 128],
                             ALU.mult, ALU.add)
                    use_bank(bi_, "dv", vvs)
                    rows0 = wd * g + tc * 128
                    wait("sy" if sqq == "sy" else "ac", cdv, vvs)
                    v_stage_last = dma(
                        sqq, s_stqp[sslot],
                        stage_v_bt[bi].ap()[rows0:rows0 + 128,
                                            128 * hpw:128 * (hpw + 1)],
                        stq[:, sq0:sq0 + 128])
                    stq_guard[sslot] = v_stage_last
            if bi == NB2 - 1:
                wait("sy", cdv, vnrm)
                v_nst = dma("sy", s_gn, stage_nrm, qkmax[0:1, 0:2])
                wait("gp", s_gn, v_nst)
                v_cn = coll(stage_nrm, g_nrm_t)
            for si in range(6):
                wait("gp", s_stqp[si], s_stqp[si].v)
            coll(stage_qk_bt[bi].ap(), g_qk_bt[bi])
            v_cb = coll(stage_v_bt[bi].ap(), g_v_bt[bi])
            wait("gp", ccc, v_cb)
            t0b = cs - 2
            emit_recopy(
                myqk_bt[bi].ap().rearrange("a b -> (a b)"),
                g_qk_bt[bi], 2048 * wd, 512 * wd, 512 * wd)
            v_myv = emit_recopy(
                myv_bt[bi].ap().rearrange("a b -> (a b)"),
                g_v_bt[bi], RANKS * wd * 256, wd * 256, wd * 256)

        if phases < 3:
            emit_all()
            return nc
        wait("sy", ccc, v_cn)
        v_gn = dma("sy", s_gn, gnrm_sb[0:1, 0:2 * RANKS],
                   bass.AP(g_nrm_t, 0, [[2 * RANKS, 1], [1, 2 * RANKS]]))
        wait("dv", s_gn, v_gn)
        dv("tensor_max", qkg[0:1, 0:2], gnrm_sb[0:1, 0:2], gnrm_sb[0:1, 2:4])
        dv("tensor_max", qkg2[0:1, 0:2], gnrm_sb[0:1, 4:6],
           gnrm_sb[0:1, 6:8])
        dv("tensor_max", qkg[0:1, 0:2], qkg[0:1, 0:2], qkg2[0:1, 0:2])
        vb1 = dv("tensor_mul", nb_s[0:1, 0:1], qkg[0:1, 0:1], qkg[0:1, 1:2])
        wait("ac", cdv, vb1)
        vb2 = act(nb_s[0:1, 0:1], nb_s[0:1, 0:1], AF.Ln)
        wait("dv", cac, vb2)
        vb3 = dv("tensor_scalar", nb_s[0:1, 0:1], nb_s[0:1, 0:1], 0.5, -LN8,
                 ALU.mult, ALU.add)
        wait("ac", cdv, vb3)
        vb4 = act(nb_s[0:1, 0:1], nb_s[0:1, 0:1], AF.Exp)
        wait("dv", cac, vb4)
        vb5 = dv("tensor_scalar_mul", nb_s[0:1, 0:1], nb_s[0:1, 0:1], -1.0)
        wait("pe", cdv, vb5)
        wait_bank("pe", 2)
        vb6 = mm(bank(2, 1), sonesr[:, :], nb_s[0:1, 0:1], True, True,
                 inc=True)
        wait("dv", cpe, vb6)
        vb7 = dv("tensor_copy", negb[:, :], bank(2, 1))
        use_bank(2, "dv", vb7)

        if phases < 4:
            emit_all()
            return nc
        s_user = {0: ("ac", 0), 1: ("ac", 0)}
        pbuf_user = {0: 0, 1: 0}
        kvb_user = {i: 0 for i in range(4)}
        vzbf_prev = {0: 0, 1: 0}
        v_qblk_user = 0
        wait("ac", cdv, vb7)
        wait("sy", s_cp, v_myv)
        wait("ac", s_cp, v_myv)
        v_vones = 0
        for i in range(4):
            vb0 = 4 * KT + 144 * i
            dv("memset", kvbuf[:, vb0 + 64: vb0 + 65], 1.0)
            v_vones = dv("memset", kvbuf[:, vb0 + 72 + 64: vb0 + 72 + 65], 1.0)
        wait("sy", cdv, v_vones)
        first_att = True

        def emit_pv(kt, nkt, diag0, vsc, vexp, vmask, vpv, zpar):
            par = kt % 2
            idx = kt % 4
            vb0 = 4 * KT + 144 * idx
            zA = 2048 + 1024 * zpar
            wait("pe", cac, vexp[kt])
            if kt >= diag0:
                wait("pe", cpl, vmask[kt])
            if kt == 0:
                wait("pe", cdv, vzbf_prev[zpar])
            first, lastk = kt == 0, kt == nkt - 1
            pa = pbuf[:, par * 2 * QB: par * 2 * QB + QB]
            pb = pbuf[:, par * 2 * QB + QB: (par + 1) * 2 * QB]
            mm(PS[0:65, zA:zA + QB], kvbuf[:, vb0: vb0 + 65],
               pa, first, lastk)
            vpv[kt] = mm(PS[0:65, zA + 512:zA + 512 + QB],
                         kvbuf[:, vb0 + 72: vb0 + 72 + 65],
                         pb, first, lastk, inc=True)
            pbuf_user[par] = vpv[kt]
            kvb_user[idx] = max(kvb_user[idx], vpv[kt], vsc[kt])
        att_i = 0
        ATT = [(hpL, g) for hpL in range(2) for g in range(NQB)]
        v_qb = {}
        v_kv = {}
        qb_user = {0: 0, 1: 0}

        def qb_dma(bidx):
            hpL, g = ATT[bidx]
            rq = (g * QB) // TOK
            qc = (g * QB) % TOK
            bq_ = qc // 512
            par = bidx % 2
            wait("sy", cpe, qb_user[par])
            v_qb[bidx] = dma(
                "sy", s_qbp[par], qblk[:, par * QB:(par + 1) * QB],
                myqk_bt[bq_].ap()[512 * rq + 128 * hpL:
                                  512 * rq + 128 * hpL + 128,
                                  qc % 512: qc % 512 + QB])

        def kv_dma(bidx, kt):
            hpL, g = ATT[bidx]
            idx = kt % 4
            rk = (kt * KT) // TOK
            kc = (kt * KT) % TOK
            kb_ = kc // 512
            kcl = kc % 512
            wait("sy", cpe, kvb_user[idx])
            dma("sy", s_kv[idx], kvbuf[:, idx * KT:(idx + 1) * KT],
                myqk_bt[kb_].ap()[512 * rk + 256 + 128 * hpL:
                                  512 * rk + 256 + 128 * hpL + 128,
                                  kcl:kcl + KT])
            vb0 = 4 * KT + 144 * idx
            wdb = LN2B[kb_][1]
            wait("ac", cpe, kvb_user[idx])
            v_kv[(bidx, kt)] = dma(
                "ac2", s_kv[idx],
                bass.AP(kvbuf, vb0,
                        [[4 * KT + 4 * 144, 128], [72, 2], [1, 64]]),
                myv_bt[kb_].ap()[wdb * rk + kcl: wdb * rk + kcl + KT,
                                 128 * hpL:128 * (hpL + 1)]
                .rearrange("t (h d) -> t h d", h=2))

        qb_dma(0)
        kv_dma(0, 0)
        if (0 * QB + QB) // KT > 1:
            kv_dma(0, 1)
        for bidx, (hpL, g) in enumerate(ATT):
            qpar = bidx % 2
            zpar = bidx % 2
            zA = 2048 + 1024 * zpar
            nkt = (g + 1) * QB // KT
            diag0 = g * QB // KT
            vsc = {}
            vpv = {-1: 0}
            vexp = {}
            vmask = {}
            for kt in range(nkt):
                par = kt % 2
                idx = kt % 4
                sbase = 1024 * par
                who, val = s_user[par]
                wait("pe", CTRS[who], val)
                if kt == 0:
                    wait("pe", s_qbp[qpar], v_qb[bidx])
                    if first_att:
                        for b_ in range(8):
                            wait_bank("pe", b_)
                        first_att = False
                wait("pe", s_kv[idx], v_kv[(bidx, kt)])
                mm(PS[:, sbase:sbase + QB],
                   kvbuf[0:64, idx * KT:(idx + 1) * KT],
                   qblk[0:64, qpar * QB:(qpar + 1) * QB], True, True,
                   tp=(0, 0))
                vsc[kt] = mm(PS[:, sbase + QB:sbase + 2 * QB],
                             kvbuf[64:128, idx * KT:(idx + 1) * KT],
                             qblk[64:128, qpar * QB:(qpar + 1) * QB],
                             True, True, tp=(64, 0), inc=True)
                if kt > 0:
                    emit_pv(kt - 1, nkt, diag0, vsc, vexp, vmask, vpv, zpar)
                wait("ac", cpe, vsc[kt])
                vexp[kt] = act(pbuf[:, par * 2 * QB:(par + 1) * 2 * QB],
                               PS[:, sbase:sbase + 2 * QB], AF.Exp,
                               bias=negb[:, 0:1], scale=0.125, chain=False)
                s_user[par] = ("ac", vexp[kt])
                if kt >= diag0:
                    wait("gp", cac, vexp[kt])
                    vmask[kt] = pl(
                        "affine_select",
                        pbuf[:, par * 2 * QB:(par + 1) * 2 * QB]
                        .rearrange("p (h q) -> p h q", h=2),
                        pbuf[:, par * 2 * QB:(par + 1) * 2 * QB]
                        .rearrange("p (h q) -> p h q", h=2),
                        [[0, 2], [1, QB]], ALU.is_ge, 0.0,
                        base=g * QB - kt * KT, channel_multiplier=-1)
                if kt + 2 < nkt:
                    kv_dma(bidx, kt + 2)
                if kt + 3 == nkt and bidx + 1 < len(ATT):
                    qb_dma(bidx + 1)
                    kv_dma(bidx + 1, 0)
                elif kt + 2 == nkt and bidx + 1 < len(ATT):
                    g2 = ATT[bidx + 1][1]
                    if nkt < 3:
                        qb_dma(bidx + 1)
                        kv_dma(bidx + 1, 0)
                    if (g2 + 1) * QB // KT > 1:
                        kv_dma(bidx + 1, 1)

            emit_pv(nkt - 1, nkt, diag0, vsc, vexp, vmask, vpv, zpar)
            qb_user[qpar] = vsc[nkt - 1]
            if nkt == 1 and bidx + 1 < len(ATT):
                qb_dma(bidx + 1)
                kv_dma(bidx + 1, 0)
                g2 = ATT[bidx + 1][1]
                if (g2 + 1) * QB // KT > 1:
                    kv_dma(bidx + 1, 1)
            wait("dv", cpe, vpv[nkt - 1])
            dv("tensor_copy", scale_sb[0:1, 0:QB],
               PS[64:65, zA:zA + QB])
            vcps = dv("tensor_copy", scale_sb[0:1, QB:2 * QB],
                      PS[64:65, zA + 512:zA + 512 + QB])
            wait("pe", cdv, vcps)
            mm(PS[64:128, zA:zA + QB], sonesr[0:1, 0:64],
               scale_sb[0:1, 0:QB], True, True)
            vbc = mm(PS[64:128, zA + 512:zA + 512 + QB], sonesr[0:1, 0:64],
                     scale_sb[0:1, QB:2 * QB], True, True, inc=True)
            wait("dv", cpe, vbc)
            vrc = dv("reciprocal", scale_sb[0:64, 0:QB],
                     PS[64:128, zA:zA + QB])
            vrc2 = dv("reciprocal", scale_sb[0:64, QB:2 * QB],
                      PS[64:128, zA + 512:zA + 512 + QB])
            dv("tensor_mul", zbf[0:64, 0:QB], PS[0:64, zA:zA + QB],
               scale_sb[0:64, 0:QB])
            vz = dv("tensor_mul", zbf[0:64, QB:2 * QB],
                    PS[0:64, zA + 512:zA + 512 + QB],
                    scale_sb[0:64, QB:2 * QB])
            vzbf_prev[zpar] = vz
            use_bank(4 + 2 * zpar, "dv", vz)
            use_bank(5 + 2 * zpar, "dv", vz)
            wait("sy", cdv, vz)
            dma("sy", s_zst,
                z_dram[64 * (2 * hpL): 64 * (2 * hpL) + 64,
                       QB * g: QB * (g + 1)], zbf[0:64, 0:QB])
            v_zstage = dma(
                "sy", s_zst,
                z_dram[64 * (2 * hpL + 1): 64 * (2 * hpL + 1) + 64,
                       QB * g: QB * (g + 1)], zbf[0:64, QB:2 * QB])
            wait("dv", s_zst, v_zstage)
        for par in range(2):
            who, val = s_user[par]
            use_bank(2 * par, who, val)
            use_bank(2 * par + 1, who, val)

        if phases < 5:
            emit_all()
            return nc
        wait("pe", s_wpj, v_wpr)
        zt_sb = xrb
        HB = TOK // 4
        MB = min(256, HB)
        v_x2add = 0
        for hloc in range(4):
            cs = 2 + HB * hloc
            for mb in range(HB // MB):
                wait("sy", cpe, st_["ffn_pe"])
                wait("sy", s_zst, s_zst.v)
                dma("sy", s_gz, zt_sb[0:64, 0:16 * MB],
                    z_dram[64 * hloc:64 * hloc + 64,
                           16 * MB * mb:16 * MB * (mb + 1)])
                vgz = dma("sy", s_gz, zt_sb[64:128, 0:16 * MB],
                          z_dram[64 * hloc:64 * hloc + 64,
                                 16 * MB * mb:16 * MB * (mb + 1)])
                wait("pe", s_gz, vgz)
                for ec in range(NCH):
                    bA = 4 + (ec % 2)
                    bB = 6 + (ec % 2)
                    wait_bank("pe", bA)
                    wait_bank("pe", bB)
                    for c in range(NCH):
                        first, last = c == 0, c == NCH - 1
                        base = c * E + 128 * ec
                        mm(bank(bA, MB), projw[0:64, base:base + 128],
                           zt_sb[0:64, 2 * c: 2 * c + 16 * (MB - 1) + 1: 16],
                           first, last, tp=(0, 0))
                        vp = mm(bank(bB, MB), projw[64:128, base:base + 128],
                                zt_sb[64:128,
                                      2 * c + 1: 2 * c + 16 * (MB - 1) + 2: 16],
                                first, last, tp=(64, 0), inc=last)
                    ccol = cs + MB * mb
                    wait("dv", cpe, vp)
                    dv("tensor_add",
                       xs[:, ec * COLS + ccol: ec * COLS + ccol + MB],
                       xs[:, ec * COLS + ccol: ec * COLS + ccol + MB],
                       bank(bA, MB))
                    v_x2add = dv("tensor_add",
                                 xs[:, ec * COLS + ccol: ec * COLS + ccol + MB],
                                 xs[:, ec * COLS + ccol: ec * COLS + ccol + MB],
                                 bank(bB, MB))
                    use_bank(bA, "dv", v_x2add)
                    use_bank(bB, "dv", v_x2add)
                st_["ffn_pe"] = max(st_["ffn_pe"], vp)
        st_["xs_dv"] = v_x2add

        v_hst = 0
        wait("sy", cdv, v_x2add)
        for c in range(NCH):
            v_hst = dma("sy", s_zst,
                        stage_x2h[c:c + 1, :],
                        xs[:, c * COLS + COLS - 1: c * COLS + COLS])
        wait("gp", s_zst, v_hst)
        v_ch = coll(stage_x2h, g_x2h_t)
        wait("gp", ccc, v_ch)
        uidh = reg_uid[0]
        reg_uid[0] += 1

        def f_h(G, uid=uidh):
            with G.register(f"ra{uid}") as ra:
                G.reg_load(ra, srnk[0:1, 0:1])
                G.reg_add(ra, ra, 3)
                G.reg_alu(ra, ra, 3, ALU.bitwise_and)
                G.reg_mul(ra, ra, 8 * 128)
                G.dma_start(out=myx2h.rearrange("a b -> (a b)"),
                            in_=bass.AP(g_x2h_t, ra, [[1, 8 * 128]])
                            ).then_inc(s_cp.sem, 16)
        on("gp", f_h)
        s_cp.bump()
        v_hcp = s_cp.v
        wait("sy", s_cp, v_hcp)
        v_hld = 0
        for c in range(NCH):
            v_hld = dma("sy", s_gz, xs[:, c * COLS + 1: c * COLS + 2],
                        myx2h[c:c + 1, :])
        v_s3 = emit_ln_stats(*LN3B[1])
        v_b3 = emit_ln_rstd(*LN3B[1], v_s3)
        emit_ln_apply(*LN3B[1], v_b3)
        wait("dv", s_gz, v_hld)
        wait("ac", s_gz, v_hld)
        wait("pe", s_gz, v_hld)
        vs_h = emit_ln_stats(1, 1, yb=ylast, ystride=1)
        vb_h = emit_ln_rstd(1, 1, vs_h)
        emit_ln_apply(1, 1, vb_h, halo_mask=True, yb=ylast, ystride=1)

        v_tr_dv = 0
        v_outst = 0
        for bi in range(1, len(LN3B)):
            cs, wd = LN3B[bi]
            nxt = LN3B[bi + 1] if bi + 1 < len(LN3B) else None
            v_x3 = emit_ffn_block(cs, wd, cs, next_ln=nxt)
            ntc = wd // 128
            for tc in range(ntc):
                c0 = cs + tc * 128
                trow = c0 - 2
                for half in range(2):
                    bi_ = 6 + half
                    wait_bank("pe", bi_)
                    wait("pe", cdv, v_x3)
                    for j in range(4):
                        ecp = 4 * half + j
                        vt = trans(
                            PS[:, 512 * bi_ + 128 * j:
                               512 * bi_ + 128 * j + 128],
                            xs[:, ecp * COLS + c0: ecp * COLS + c0 + 128],
                            inc=(j == 3))
                    wait("ac", cpe, vt)
                    wait("ac", s_out, v_outst)
                    v_tr_ac = act(outsb[:, :], bank(bi_, 512), AF.Copy,
                                  chain=False)
                    use_bank(bi_, "ac", v_tr_ac)
                    wait("ac", cac, v_tr_ac)
                    v_outst = dma("sc", s_out,
                                  out[trow:trow + 128,
                                      512 * half:512 * (half + 1)],
                                  outsb[:, :])

        on("ac", lambda A, v=s_out.v: A.wait_ge(s_out.sem, v))

        emit_all()
    return nc


def _noop():
    pass


def _build_emit(block, ops):
        @block.sync
        def _(sync):
            for fn in ops["sy"]:
                fn(sync)

        @block.gpsimd
        def _(gpsimd):
            for fn in ops["gp"]:
                fn(gpsimd)

        @block.tensor
        def _(tensor):
            for fn in ops["pe"]:
                fn(tensor)

        @block.scalar
        def _(scalar):
            for fn in ops["ac"]:
                fn(scalar)

        @block.vector
        def _(vector):
            for fn in ops["dv"]:
                fn(vector)


def make_inputs(x_full, Wq, bq, Wk, bk, Wv, bv, proj, W_r, W_k, W_v, mu_r,
                mu_k, TOK=2048):
    f32 = np.float32
    WqT = np.ascontiguousarray(np.asarray(Wq).reshape(H * DH, E).T.astype(f32))
    WkTa = np.ascontiguousarray(np.asarray(Wk).reshape(H * DH, E).T.astype(f32))
    WvTa = np.ascontiguousarray(np.asarray(Wv).reshape(H * DH, E).T.astype(f32))
    wpr = np.ascontiguousarray(np.asarray(proj).astype(f32))
    wrT = np.ascontiguousarray(np.asarray(W_r).T.astype(f32))
    wkT = np.ascontiguousarray(np.asarray(W_k).T.astype(f32))
    wvT = np.ascontiguousarray(np.asarray(W_v).T.astype(f32))
    bq_t = np.ascontiguousarray(np.asarray(bq).reshape(8, 128).T.astype(f32))
    bk_t = np.ascontiguousarray(np.asarray(bk).reshape(8, 128).T.astype(f32))
    bv_b = np.ascontiguousarray(np.broadcast_to(
        np.asarray(bv).reshape(1, H * DH), (128, H * DH)).astype(f32))
    mur_t = np.ascontiguousarray(np.asarray(mu_r).reshape(8, 128).T.astype(f32))
    muk_t = np.ascontiguousarray(np.asarray(mu_k).reshape(8, 128).T.astype(f32))
    id_ = np.eye(128, dtype=f32)
    onesr_ = np.ones((1, 128), dtype=f32)
    onescb_ = np.ones((128, 1), dtype=ml_dtypes.bfloat16)
    consts_ = np.array([[1e-5, 0.0, 0.0, 0.0]], dtype=f32)

    in_maps = []
    for core in range(8):
        b, i = core // RANKS, core % RANKS
        s0 = i * TOK
        xsl = np.zeros((TOK + 2, E), dtype=f32)
        lo = max(0, s0 - 2)
        xsl[2 - (s0 - lo):] = x_full[b, lo:s0 + TOK]
        xin_ = np.ascontiguousarray(xsl.T)
        hm = np.full((128, 1), 0.0 if i == 0 else 1.0, dtype=f32)
        in_maps.append(dict(
            xin=xin_, wrT=wrT, wkT=wkT, wvT=wvT, wqaT=WqT, wkaT=WkTa,
            wvaT=WvTa, wpr=wpr, bq_t=bq_t, bk_t=bk_t, bv_bc=bv_b,
            mur_t=mur_t, muk_t=muk_t, hmask=hm, id128=id_, onesr=onesr_,
            onescb=onescb_, consts=consts_,
            rnk=np.array([[i]], dtype=np.int32)))
    return in_maps


def kernel(L0_attn_x, L0_attn_Wq, L0_attn_bq, L0_attn_Wk, L0_attn_bk,
           L0_attn_Wv, L0_attn_bv, L0_attn_proj, L0_ffn_W_r, L0_ffn_W_k,
           L0_ffn_W_v, L0_ffn_mu_r, L0_ffn_mu_k):
    TOK = 2048
    x = np.asarray(L0_attn_x, dtype=np.float32)
    in_maps = make_inputs(
        x, L0_attn_Wq, L0_attn_bq, L0_attn_Wk, L0_attn_bk, L0_attn_Wv,
        L0_attn_bv, L0_attn_proj, L0_ffn_W_r, L0_ffn_W_k, L0_ffn_W_v,
        L0_ffn_mu_r, L0_ffn_mu_k, TOK=TOK)
    nc = build_nc(TOK=TOK, QB=512)
    res = run_bass_kernel_spmd(nc, in_maps, core_ids=list(range(8)))
    outf = np.empty((2, RANKS * TOK, E), dtype=np.float32)
    for core in range(8):
        b, i = core // RANKS, core % RANKS
        outf[b, i * TOK:(i + 1) * TOK] = res.results[core]["out"]
    return outf


# version 22 (final)
# speedup vs baseline: 1.0420x; 1.0203x over previous
import numpy as np
import ml_dtypes
from contextlib import ExitStack

import concourse.bass as bass
import concourse.mybir as mybir
from concourse.bass_utils import run_bass_kernel_spmd

F32 = mybir.dt.float32
BF16 = mybir.dt.bfloat16
I32 = mybir.dt.int32
AF = mybir.ActivationFunctionType
ALU = mybir.AluOpType
AX = mybir.AxisListType

E = 1024
NCH = 8
H = 16
DH = 64
RANKS = 4
BW = 512
KT = 128
LN8 = 2.0794415416798357


def build_nc(TOK=2048, QB=512, phases=9, att_n=10**9):
    S = RANKS * TOK
    COLS = TOK + 2
    NQB = S // QB
    QBW = TOK // QB
    assert TOK % QB == 0 and QB % KT == 0 and 128 <= QB <= 512

    nc = bass.Bass("TRN2", target_bir_lowering=False)

    xin = nc.dram_tensor("xin", [E, COLS], F32, kind="ExternalInput").ap()
    wrT = nc.dram_tensor("wrT", [E, E], F32, kind="ExternalInput").ap()
    wkT = nc.dram_tensor("wkT", [E, E], F32, kind="ExternalInput").ap()
    wvT = nc.dram_tensor("wvT", [E, E], F32, kind="ExternalInput").ap()
    wqaT = nc.dram_tensor("wqaT", [E, E], F32, kind="ExternalInput").ap()
    wkaT = nc.dram_tensor("wkaT", [E, E], F32, kind="ExternalInput").ap()
    wvaT = nc.dram_tensor("wvaT", [E, E], F32, kind="ExternalInput").ap()
    wpr = nc.dram_tensor("wpr", [E, E], F32, kind="ExternalInput").ap()
    bq_t = nc.dram_tensor("bq_t", [128, 8], F32, kind="ExternalInput").ap()
    bk_t = nc.dram_tensor("bk_t", [128, 8], F32, kind="ExternalInput").ap()
    bv_bc = nc.dram_tensor("bv_bc", [128, E], F32, kind="ExternalInput").ap()
    mur_t = nc.dram_tensor("mur_t", [128, 8], F32, kind="ExternalInput").ap()
    muk_t = nc.dram_tensor("muk_t", [128, 8], F32, kind="ExternalInput").ap()
    hmask = nc.dram_tensor("hmask", [128, 1], F32, kind="ExternalInput").ap()
    id128 = nc.dram_tensor("id128", [128, 128], F32, kind="ExternalInput").ap()
    onesr = nc.dram_tensor("onesr", [1, 128], F32, kind="ExternalInput").ap()
    onescb = nc.dram_tensor("onescb", [128, 1], BF16, kind="ExternalInput").ap()
    consts = nc.dram_tensor("consts", [1, 4], F32, kind="ExternalInput").ap()
    rnk = nc.dram_tensor("rnk", [1, 1], I32, kind="ExternalInput").ap()
    out = nc.dram_tensor("out", [TOK, E], F32, kind="ExternalOutput").ap()

    qa_bf = nc.dram_tensor("qa_bf", [E, E], BF16).ap()
    ka_bf = nc.dram_tensor("ka_bf", [E, E], BF16).ap()
    va_bf = nc.dram_tensor("va_bf", [E, E], BF16).ap()


    z_dram = nc.dram_tensor("z_dram", [256, S], BF16).ap()
    stage_x2h = nc.dram_tensor("stage_x2h", [8, 128], F32).ap()
    g_x2h_t = nc.dram_tensor("g_x2h", [RANKS * 8, 128], F32)
    myx2h = nc.dram_tensor("myx2h", [8, 128], F32).ap()
    stage_nrm = nc.dram_tensor("stage_nrm", [1, 2], F32).ap()
    g_nrm_t = nc.dram_tensor("g_nrm", [RANKS, 2], F32)


    def blocks(lo):
        out_ = []
        c = lo
        while c < COLS:
            if c < 2:
                w = 2 - c
            else:
                w = min(BW - ((c - 2) % BW), COLS - c)
            out_.append((c, w))
            c += w
        return out_

    LN1B = blocks(0)
    FFN1B = blocks(1)
    LN2B = blocks(2)
    LN3B = blocks(1)
    stage_qk_bt = [nc.dram_tensor(f"stage_qk{b}", [RANKS * 512, w], BF16)
                   for b, (c, w) in enumerate(LN2B)]
    g_qk_bt = [nc.dram_tensor(f"g_qk{b}", [RANKS * RANKS * 512, w], BF16)
               for b, (c, w) in enumerate(LN2B)]
    stage_v_bt = [nc.dram_tensor(f"stage_v{b}", [RANKS * w, 256], BF16)
                  for b, (c, w) in enumerate(LN2B)]
    g_v_bt = [nc.dram_tensor(f"g_v{b}", [RANKS * RANKS * w, 256], BF16)
              for b, (c, w) in enumerate(LN2B)]
    myqk_bt = [nc.dram_tensor(f"myqk{b}", [RANKS * 512, w], BF16)
               for b, (c, w) in enumerate(LN2B)]
    myv_bt = [nc.dram_tensor(f"myv{b}", [RANKS * w, 256], BF16)
              for b, (c, w) in enumerate(LN2B)]
    NB2 = len(LN2B)

    with ExitStack() as ctx:
        en = ctx.enter_context
        xs = en(nc.sbuf_tensor("xs", [128, NCH * COLS], F32))
        ysb = en(nc.sbuf_tensor("ysb", [128, NCH * BW], BF16))
        ylast = en(nc.sbuf_tensor("ylast", [128, NCH], BF16))
        scr32 = en(nc.sbuf_tensor("scr32", [128, BW], F32))
        xrb = en(nc.sbuf_tensor("xrb", [128, NCH * BW], BF16))
        xkb = en(nc.sbuf_tensor("xkb", [128, NCH * BW], BF16))
        rscr = en(nc.sbuf_tensor("rscr", [128, 2 * BW], BF16))
        kact = en(nc.sbuf_tensor("kact", [128, NCH * BW], BF16))
        wr = en(nc.sbuf_tensor("wr", [128, NCH * E], BF16))
        wk = en(nc.sbuf_tensor("wk", [128, NCH * E], BF16))
        wv = en(nc.sbuf_tensor("wv", [128, NCH * E], BF16))
        qkvw = en(nc.sbuf_tensor("qkvw", [128, 2 * 3 * NCH * 128], BF16))
        projw = en(nc.sbuf_tensor("projw", [128, NCH * E], BF16))
        qblk = en(nc.sbuf_tensor("qblk", [128, 2 * QB], BF16))
        kvbuf = en(nc.sbuf_tensor("kvbuf", [128, 4 * KT + 4 * 144], BF16))
        pbuf = en(nc.sbuf_tensor("pbuf", [128, 2 * 2 * QB], BF16))
        zbf = en(nc.sbuf_tensor("zbf", [64, 2 * QB], BF16))
        scale_sb = en(nc.sbuf_tensor("scale_sb", [64, 2 * QB], F32))
        sqt = en(nc.sbuf_tensor("sqt", [128, BW], BF16))
        outsb = en(nc.sbuf_tensor("outsb", [128, 512], F32))
        sbias_q = en(nc.sbuf_tensor("sbias_q", [128, 8], F32))
        sbias_k = en(nc.sbuf_tensor("sbias_k", [128, 8], F32))
        sbv8 = en(nc.sbuf_tensor("sbv8", [128, E], BF16))
        smur = en(nc.sbuf_tensor("smur", [128, 8], F32))
        smuk = en(nc.sbuf_tensor("smuk", [128, 8], F32))
        shm = en(nc.sbuf_tensor("shm", [128, 1], F32))
        sid = en(nc.sbuf_tensor("sid", [128, 128], F32))
        sonesr = en(nc.sbuf_tensor("sonesr", [1, 128], F32))
        sonescb = en(nc.sbuf_tensor("sonescb", [128, 1], BF16))
        sconsts = en(nc.sbuf_tensor("sconsts", [1, 4], F32))
        srnk = en(nc.sbuf_tensor("srnk", [1, 1], I32))
        m_sb = en(nc.sbuf_tensor("m_sb", [1, BW], F32))
        v_sb = en(nc.sbuf_tensor("v_sb", [1, BW], F32))
        v2_sb = en(nc.sbuf_tensor("v2_sb", [1, BW], F32))
        rstd_sb = en(nc.sbuf_tensor("rstd_sb", [1, BW], F32))
        qkmax = en(nc.sbuf_tensor("qkmax", [1, 2], F32))
        nrm_s = en(nc.sbuf_tensor("nrm_s", [1, 1], F32))
        gnrm_sb = en(nc.sbuf_tensor("gnrm_sb", [1, RANKS * 2], F32))
        qkg = en(nc.sbuf_tensor("qkg", [1, 2], F32))
        qkg2 = en(nc.sbuf_tensor("qkg2", [1, 2], F32))
        negb = en(nc.sbuf_tensor("negb", [128, 1], F32))
        nb_s = en(nc.sbuf_tensor("nb_s", [1, 1], F32))

        PS = en(nc.psum_tensor("PS", [128, 4096], F32))

        class Ctr:
            def __init__(self, name, step=1):
                self.sem = en(nc.semaphore(name))
                self.step = step
                self.v = 0

            def bump(self):
                self.v += self.step
                return self.v

        s_ld = Ctr("s_ld", 16)
        s_wffn = Ctr("s_wffn", 16)
        s_wqc = Ctr("s_wqc", 16)
        s_wpj = Ctr("s_wpj", 16)
        s_wq = [Ctr("s_wq0", 16), Ctr("s_wq1", 16)]
        s_stqp = [Ctr(f"s_stq{i}", 16) for i in range(6)]
        s_qbp = [Ctr("s_qb0", 16), Ctr("s_qb1", 16)]
        s_kv = [Ctr(f"s_kv{i}", 16) for i in range(4)]
        s_zst = Ctr("s_zst", 16)
        s_gn = Ctr("s_gn", 16)
        s_gz = Ctr("s_gz", 16)
        s_out = Ctr("s_out", 16)
        s_cp = Ctr("s_cp", 16)
        s_bv = Ctr("s_bv", 16)
        cpe = Ctr("cpe")
        cac = Ctr("cac")
        cdv = Ctr("cdv")
        cpl = Ctr("cpl")
        ccc = Ctr("ccc")
        CTRS = {"pe": cpe, "ac": cac, "dv": cdv, "pl": cpl}

        block = en(nc.Block())
        ops = {"sy": [], "gp": [], "pe": [], "ac": [], "dv": []}

        def emit_all():
            _build_emit(block, ops)

        def on(q, fn):
            ops[q].append(fn)

        def wait(q, ctr, val):
            if val and val > 0:
                on(q, lambda EG, s=ctr.sem, v=val: EG.wait_ge(s, v))

        def dma(q, c, out_ap, in_ap, **kw):
            eq = {"sy": "sy", "gp": "gp", "sc": "ac", "ac2": "ac",
                  "vd": "dv"}[q]
            c.bump()
            on(eq, lambda EG, o=out_ap, i=in_ap, s=c.sem, kw=kw:
               EG.dma_start(out=o, in_=i, **kw).then_inc(s, 16))
            return c.v

        def mm(out_ap, lhsT, rhs, start, stop, tp=None, inc=False):
            if inc:
                cpe.bump()
            v = cpe.v

            def f(T, o=out_ap, l=lhsT, r=rhs, st=start, sp=stop, tp=tp,
                  inc=inc):
                i = T.matmul(o, l, r, start=st, stop=sp, tile_position=tp)
                if inc:
                    i.then_inc(cpe.sem, 1)
            on("pe", f)
            return v

        def trans(out_ap, in_ap, inc=False):
            if inc:
                cpe.bump()
            v = cpe.v

            def f(T, o=out_ap, i=in_ap, inc=inc):
                r = T.transpose(o, i, sid[:, :])
                if inc:
                    r.then_inc(cpe.sem, 1)
            on("pe", f)
            return v

        def act(out_ap, in_ap, func, bias=0.0, scale=1.0, chain=True):
            if chain:
                wait("ac", cac, cac.v)
            v = cac.bump()

            def f(A, o=out_ap, i=in_ap, fn=func, b=bias, s=scale):
                A.activation(o, i, fn, bias=b, scale=s).then_inc(cac.sem, 1)
            on("ac", f)
            return v

        def dv(name, *args, chain=True, **kw):
            if chain:
                wait("dv", cdv, cdv.v)
            v = cdv.bump()

            def f(V, n=name, a=args, kw=kw):
                getattr(V, n)(*a, **kw).then_inc(cdv.sem, 1)
            on("dv", f)
            return v

        def pl(name, *args, **kw):
            wait("gp", cpl, cpl.v)
            v = cpl.bump()

            def f(G, n=name, a=args, kw=kw):
                getattr(G, n)(*a, **kw).then_inc(cpl.sem, 1)
            on("gp", f)
            return v

        def bank(i, w=512, rows=(0, 128)):
            return PS[rows[0]:rows[1], 512 * i: 512 * i + w]

        last_bank_use = {i: ("dv", 0) for i in range(8)}

        def use_bank(i, who, val):
            last_bank_use[i] = (who, val)

        def wait_bank(q, i):
            who, val = last_bank_use[i]
            wait(q, CTRS[who], val)

        v_x = dma("sy", s_ld, xs[:, :].rearrange("p (c t) -> p c t", c=NCH),
                  xin.rearrange("(c p) t -> p c t", p=128))
        for dst, src in [(sbias_q[:, :], bq_t), (sbias_k[:, :], bk_t),
                         (smur[:, :], mur_t),
                         (smuk[:, :], muk_t), (shm[:, :], hmask),
                         (sid[:, :], id128), (sonesr[:, :], onesr),
                         (sonescb[:, :], onescb), (sconsts[:, :], consts),
                         (srnk[:, :], rnk)]:
            v_small = dma("sy", s_ld, dst, src)
        v_wffn = 0
        for dst, src in [(wr, wrT), (wk, wkT), (wv, wvT)]:
            v_wffn = dma("gp", s_wffn,
                         dst[:, :].rearrange("p (c d) -> p c d", c=NCH),
                         src.rearrange("(c p) d -> p c d", p=128))
        v_qc = 0
        for dst, src in [(qa_bf, wqaT), (ka_bf, wkaT), (va_bf, wvaT)]:
            for hh in range(8):
                v_qc = dma("gp", s_wqc,
                           dst[hh * 128:(hh + 1) * 128, :]
                           .rearrange("p (c d) -> p c d", d=128),
                           src[:, 128 * hh:128 * (hh + 1)]
                           .rearrange("(c p) d -> p c d", p=128))
        v_wpr = dma("gp", s_wpj,
                    projw[:, :].rearrange("p (c d) -> p c d", c=NCH),
                    wpr.rearrange("(c p) d -> p c d", p=128))
        v_bv = dma("gp", s_bv, sbv8[:, :], bv_bc)
        wait("dv", s_ld, v_small)
        wait("dv", s_bv, v_bv)
        v_bv8 = dv("tensor_scalar_mul", sbv8[:, :], sbv8[:, :], 0.125)

        st_ = {"ln_apply": 0,
               "ysb_pe": {0: 0, 1: 0},
               "ffn_pe": 0,
               "ffn_rscr": 0,
               "sq_ac": 0,
               "kact_ac": 0,
               "sqt_pe": 0,
               "lnscr_pe": 0,
               "xs_dv": 0}

        def emit_ln_stats(cs, wd, yb=None, ystride=BW, ybi=0):
            yb = ysb if yb is None else yb
            v_sum = 0
            for c in range(NCH):
                xc = xs[:, c * COLS + cs: c * COLS + cs + wd]
                if c == 0:
                    wait("dv", cdv, st_["ln_apply"])
                    g = st_["ysb_pe"][ybi]
                    if ybi == 1:
                        g = max(g, st_["ffn_pe"])
                    wait("dv", cpe, g)
                    wait("ac", cdv, st_["xs_dv"])
                vcp = dv("tensor_copy", yb[:, c * ystride: c * ystride + wd],
                         xc)
                wait("ac", cpe, st_["sqt_pe"])
                vsq = act(sqt[:, 0:wd], xc, AF.Square)
                if c == 0:
                    wait_bank("pe", 0)
                    wait_bank("pe", 1)
                wait("pe", cdv, vcp)
                mm(bank(0, wd, (0, 1)), sonescb[:, :],
                   yb[:, c * ystride: c * ystride + wd], c == 0,
                   c == NCH - 1)
                wait("pe", cac, vsq)
                v_sum = mm(bank(1, wd, (0, 1)), sonescb[:, :], sqt[:, 0:wd],
                           c == 0, c == NCH - 1, inc=True)
                st_["sqt_pe"] = v_sum
            st_["sq_ac"] = vsq
            st_["ysb_pe"][ybi] = v_sum
            return v_sum

        def emit_ln_rstd(cs, wd, v_sum):
            wait("dv", cpe, max(v_sum, st_["lnscr_pe"]))
            vm = dv("tensor_scalar_mul", m_sb[0:1, 0:wd],
                    bank(0, wd, (0, 1)), 1.0 / E)
            ve1 = dv("tensor_scalar_mul", v_sb[0:1, 0:wd],
                     bank(1, wd, (0, 1)), 1.0 / E)
            use_bank(0, "dv", vm)
            use_bank(1, "dv", ve1)
            vm2 = dv("tensor_mul", v2_sb[0:1, 0:wd], m_sb[0:1, 0:wd],
                     m_sb[0:1, 0:wd])
            vvar = dv("tensor_sub", v_sb[0:1, 0:wd], v_sb[0:1, 0:wd],
                      v2_sb[0:1, 0:wd])
            wait("ac", cdv, vvar)
            wait("ac", cpe, st_["lnscr_pe"])
            vln = act(rstd_sb[0:1, 0:wd], v_sb[0:1, 0:wd], AF.Ln,
                      bias=sconsts[0:1, 0:1])
            wait("dv", cac, vln)
            vt = dv("tensor_scalar_mul", rstd_sb[0:1, 0:wd],
                    rstd_sb[0:1, 0:wd], -0.5)
            wait("ac", cdv, vt)
            vex = act(rstd_sb[0:1, 0:wd], rstd_sb[0:1, 0:wd], AF.Exp)
            wait_bank("pe", 2)
            wait_bank("pe", 3)
            wait("pe", cdv, vm2)
            mm(bank(2, wd), sonesr[:, :], m_sb[0:1, 0:wd], True, True)
            wait("pe", cac, vex)
            vbc = mm(bank(3, wd), sonesr[:, :], rstd_sb[0:1, 0:wd], True,
                     True, inc=True)
            st_["lnscr_pe"] = vbc
            return vbc

        def emit_ln_apply(cs, wd, vbc, chunks=None, halo_mask=False,
                          yb=None, ystride=BW, ybi=0):
            yb = ysb if yb is None else yb
            chunks = range(NCH) if chunks is None else chunks
            vap = 0
            for c in chunks:
                xc = xs[:, c * COLS + cs: c * COLS + cs + wd]
                if c == 0:
                    wait("dv", cpe, vbc)
                dv("tensor_sub", scr32[:, 0:wd], xc, bank(2, wd))
                vap = dv("tensor_mul", yb[:, c * ystride: c * ystride + wd],
                         scr32[:, 0:wd], bank(3, wd))
            use_bank(2, "dv", vap)
            use_bank(3, "dv", vap)
            if halo_mask:
                vap = dv("tensor_scalar_mul",
                         yb[:, 0: (NCH - 1) * ystride + 1: ystride],
                         yb[:, 0: (NCH - 1) * ystride + 1: ystride],
                         shm[:, 0:1])
            st_["ln_apply"] = max(st_["ln_apply"], vap)
            return vap

        def emit_ln(cs, wd, halo_mask=False, yb=None, ybi=0):
            v_sum = emit_ln_stats(cs, wd, yb=yb, ybi=ybi)
            vbc = emit_ln_rstd(cs, wd, v_sum)
            return emit_ln_apply(cs, wd, vbc, halo_mask=halo_mask, yb=yb,
                                 ybi=ybi)

        def emit_ylast(lnwd):
            v = dv("tensor_copy", ylast[:, 0:NCH],
                   ysb[:, lnwd - 1: (NCH - 1) * BW + lnwd: BW])
            st_["ln_apply"] = max(st_["ln_apply"], v)
            return v

        def emit_ffn_block(cs, wd, ln_cs, next_ln=None):
            off = cs - ln_cs
            assert off in (0, 1)
            wait("dv", cpe, max(st_["ffn_pe"], st_["ysb_pe"][1]))
            vmix = 0
            for c in range(NCH):
                y0 = c * BW
                yb = ysb[:, y0: y0 + wd + off]
                if off == 1:
                    dv("tensor_sub", xrb[:, y0: y0 + wd], yb[:, 1:1 + wd],
                       yb[:, 0:wd])
                    dv("scalar_tensor_tensor", xkb[:, y0: y0 + wd],
                       xrb[:, y0: y0 + wd], smuk[:, c:c + 1], yb[:, 0:wd],
                       ALU.mult, ALU.add)
                    vmix = dv("scalar_tensor_tensor", xrb[:, y0: y0 + wd],
                              xrb[:, y0: y0 + wd], smur[:, c:c + 1],
                              yb[:, 0:wd], ALU.mult, ALU.add)
                else:
                    dv("tensor_sub", xrb[:, y0: y0 + 1], yb[:, 0:1],
                       ylast[:, c:c + 1])
                    dv("scalar_tensor_tensor", xkb[:, y0: y0 + 1],
                       xrb[:, y0: y0 + 1], smuk[:, c:c + 1],
                       ylast[:, c:c + 1], ALU.mult, ALU.add)
                    vmix = dv("scalar_tensor_tensor", xrb[:, y0: y0 + 1],
                              xrb[:, y0: y0 + 1], smur[:, c:c + 1],
                              ylast[:, c:c + 1], ALU.mult, ALU.add)
                    if wd > 1:
                        dv("tensor_sub", xrb[:, y0 + 1: y0 + wd],
                           yb[:, 1:wd], yb[:, 0:wd - 1])
                        dv("scalar_tensor_tensor", xkb[:, y0 + 1: y0 + wd],
                           xrb[:, y0 + 1: y0 + wd], smuk[:, c:c + 1],
                           yb[:, 0:wd - 1], ALU.mult, ALU.add)
                        vmix = dv("scalar_tensor_tensor",
                                  xrb[:, y0 + 1: y0 + wd],
                                  xrb[:, y0 + 1: y0 + wd], smur[:, c:c + 1],
                                  yb[:, 0:wd - 1], ALU.mult, ALU.add)
            emit_ylast(wd + off)
            n_vsum = None
            if next_ln is not None:
                n_vsum = emit_ln_stats(next_ln[0], next_ln[1])
            wait("pe", cdv, vmix)
            vkct = {}
            for d in range(NCH):
                bi_k = 6 + (d % 2)
                wait_bank("pe", bi_k)
                for c in range(NCH):
                    last = c == NCH - 1
                    vk = mm(bank(bi_k, wd),
                            wk[:, c * E + 128 * d: c * E + 128 * (d + 1)],
                            xkb[:, c * BW: c * BW + wd], c == 0, last,
                            inc=last)
                wait("dv", cpe, vk)
                if d == 0:
                    wait("dv", cac, st_["kact_ac"])
                vrl = dv("tensor_scalar_max", kact[:, d * BW: d * BW + wd],
                         bank(bi_k, wd), 0.0)
                wait("ac", cdv, vrl)
                vkct[d] = act(kact[:, d * BW: d * BW + wd],
                              kact[:, d * BW: d * BW + wd], AF.Square)
                use_bank(bi_k, "dv", vrl)
            n_vbc = None
            if next_ln is not None:
                n_vbc = emit_ln_rstd(next_ln[0], next_ln[1], n_vsum)
            st_["kact_ac"] = vkct[NCH - 1]
            wait("pe", cac, vkct[NCH - 1])
            vsig = {}
            vt2d = {}
            vout = 0
            for d in range(NCH):
                par = d % 2
                bi_r, bi_v = 4 + par, 6 + par
                wait_bank("pe", bi_r)
                for c in range(NCH):
                    last = c == NCH - 1
                    vr = mm(bank(bi_r, wd),
                            wr[:, c * E + 128 * d: c * E + 128 * (d + 1)],
                            xrb[:, c * BW: c * BW + wd], c == 0, last,
                            inc=last)
                wait("ac", cpe, vr)
                if d >= 2:
                    wait("ac", cdv, vt2d[d - 2])
                else:
                    wait("ac", cdv, st_["ffn_rscr"])
                vsig[d] = act(rscr[:, par * BW: par * BW + wd],
                              bank(bi_r, wd), AF.Sigmoid)
                use_bank(bi_r, "ac", vsig[d])
                wait_bank("pe", bi_v)
                for c in range(NCH):
                    last = c == NCH - 1
                    vv = mm(bank(bi_v, wd),
                            wv[:, c * E + 128 * d: c * E + 128 * (d + 1)],
                            kact[:, c * BW: c * BW + wd], c == 0, last,
                            inc=last)
                wait("dv", cpe, vv)
                wait("dv", cac, vsig[d])
                vt2d[d] = dv("scalar_tensor_tensor", scr32[:, 0:wd],
                             bank(bi_v, wd), 0.5,
                             rscr[:, par * BW: par * BW + wd],
                             ALU.mult, ALU.mult)
                use_bank(bi_v, "dv", vt2d[d])
                vout = dv("tensor_add",
                          xs[:, d * COLS + cs: d * COLS + cs + wd],
                          xs[:, d * COLS + cs: d * COLS + cs + wd],
                          scr32[:, 0:wd])
                if next_ln is not None:
                    emit_ln_apply(next_ln[0], next_ln[1], n_vbc, chunks=[d])
            st_["ffn_pe"] = vv
            st_["ffn_rscr"] = vt2d[NCH - 1]
            st_["xs_dv"] = vout
            return vout

        wait("pe", s_ld, v_small)
        wait("dv", s_ld, v_small)
        wait("ac", s_ld, v_small)
        wait("pe", s_wffn, v_wffn)
        v_x1 = 0
        LN2B_pre = blocks(2)
        for bi, (cs, wd) in enumerate(FFN1B):
            ln_cs, ln_wd = LN1B[bi]
            if bi == 0:
                emit_ln(ln_cs, ln_wd)
            nxt = LN1B[bi + 1] if bi + 1 < len(FFN1B) else LN2B_pre[0]
            v_x1 = emit_ffn_block(cs, wd, ln_cs, next_ln=nxt)

        if phases < 2:
            emit_all()
            return nc
        groups = [[0, 1, 2, 3], [4, 5, 6, 7]]

        def coll(ins_ap, outs_t):
            v = ccc.bump()
            on("gp", lambda G, i=ins_ap, o=outs_t.ap():
               G.collective_compute("AllGather", ALU.bypass,
                                    replica_groups=groups,
                                    ins=[i], outs=[o]).then_inc(ccc.sem, 1))
            return v

        reg_uid = [0]

        def emit_recopy(dst_ap, src_t, rankblk, blkstride, blkelems):
            uid = reg_uid[0]
            reg_uid[0] += 1

            def f(G, st=src_t, rb=rankblk, bs=blkstride, be=blkelems,
                  dst=dst_ap, uid=uid):
                with G.register(f"ra{uid}") as ra:
                    G.reg_load(ra, srnk[0:1, 0:1])
                    G.reg_mul(ra, ra, bs)
                    G.dma_start(out=dst,
                                in_=bass.AP(st, ra, [[rb, RANKS], [1, be]])
                                ).then_inc(s_cp.sem, 16)
            on("gp", f)
            s_cp.bump()
            return s_cp.v

        v_nrm0 = dv("memset", qkmax[0:1, 0:2], 0.0)
        v_stage_last = 0
        stq = kact
        stq_guard = {i: 0 for i in range(6)}
        STQQ = ["sc", "ac2", "ac2", "sc", "ac2", "sc"]
        vnrm = v_nrm0
        wslot_user = {0: 0, 1: 0}
        ysb2 = xrb
        vln2_next = 0
        for bi, (cs, wd) in enumerate(LN2B):
            ybi = bi % 2
            yb_cur = ysb if ybi == 0 else ysb2
            if bi == 0:
                vln2 = st_["ln_apply"]
            else:
                vln2 = vln2_next
            wait("pe", cdv, vln2)
            tok0 = cs - 2
            stq_i = [0]
            for hp in range(8):
                par = hp % 2
                g, hpw = hp // 2, hp % 2
                if hp == 4 and bi + 1 < NB2:
                    ncs, nwd = LN2B[bi + 1]
                    vln2_next = emit_ln(ncs, nwd,
                                        yb=(ysb if (bi + 1) % 2 == 0
                                            else ysb2),
                                        ybi=(bi + 1) % 2)
                wait("sy", cpe, wslot_user[par])
                if bi == 0 and hp < 2:
                    wait("sy", s_wqc, v_qc)
                for mi, wsrc in enumerate([qa_bf, ka_bf, va_bf]):
                    slot = qkvw[:, (par * 3 + mi) * NCH * 128:
                                (par * 3 + mi + 1) * NCH * 128]
                    vw = dma("sy", s_wq[par], slot,
                             wsrc[128 * hp:128 * (hp + 1), :])

                def wsl(mi, c, par=par):
                    base = (par * 3 + mi) * NCH * 128 + c * 128
                    return qkvw[:, base: base + 128]

                wait("pe", s_wq[par], vw)
                for mi, bias_t, rows0 in [
                        (0, sbias_q, 512 * g + 128 * hpw),
                        (1, sbias_k, 512 * g + 256 + 128 * hpw)]:
                    bi_ = 4 + mi
                    sslot = stq_i[0] % 6
                    sqq = STQQ[stq_i[0] % 6]
                    sq0 = BW * sslot
                    stq_i[0] += 1
                    wait_bank("pe", bi_)
                    for c in range(NCH):
                        last = c == NCH - 1
                        vq = mm(bank(bi_, wd), wsl(mi, c),
                                yb_cur[:, c * BW: c * BW + wd], c == 0,
                                last, inc=last)
                    wslot_user[par] = vq
                    st_["ysb_pe"][ybi] = vq
                    wait("dv", cpe, vq)
                    wait("dv", s_stqp[sslot], stq_guard[sslot])
                    vev = dv("tensor_scalar_add", stq[:, sq0:sq0 + wd],
                             bank(bi_, wd), bias_t[:, hp:hp + 1])
                    use_bank(bi_, "dv", vev)
                    wait("ac", cdv, vev)
                    wait("ac", cpe, st_["sqt_pe"])
                    vsq = act(sqt[:, 0:wd], stq[:, sq0:sq0 + wd], AF.Square)
                    st_["sq_ac"] = vsq
                    wait("pe", cac, vsq)
                    wait_bank("pe", 1)
                    vn = mm(bank(1, wd, (0, 1)), sonescb[:, :], sqt[:, 0:wd],
                            True, True, inc=True)
                    st_["sqt_pe"] = vn
                    wait("dv", cpe, vn)
                    vrm = dv("reduce_max", nrm_s[0:1, 0:1],
                             bank(1, wd, (0, 1)), axis=AX.X)
                    use_bank(1, "dv", vrm)
                    vnrm = dv("tensor_max", qkmax[0:1, mi:mi + 1],
                              qkmax[0:1, mi:mi + 1], nrm_s[0:1, 0:1])
                    wait("sy" if sqq == "sy" else "ac", cdv, vev)
                    v_stage_last = dma(
                        sqq, s_stqp[sslot],
                        stage_qk_bt[bi].ap()[rows0:rows0 + 128, 0:wd],
                        stq[:, sq0:sq0 + wd])
                    stq_guard[sslot] = v_stage_last
                ntc = wd // 128
                for tc in range(ntc):
                    bi_ = 6 + (tc % 2)
                    sslot = stq_i[0] % 6
                    sqq = STQQ[stq_i[0] % 6]
                    sq0 = BW * sslot
                    stq_i[0] += 1
                    wait_bank("pe", bi_)
                    for c in range(NCH):
                        last = c == NCH - 1
                        vv = mm(bank(bi_, 128),
                                yb_cur[:, c * BW + tc * 128:
                                       c * BW + (tc + 1) * 128],
                                wsl(2, c), c == 0, last, inc=last)
                    wslot_user[par] = vv
                    st_["ysb_pe"][ybi] = vv
                    wait("dv", cpe, vv)
                    wait("dv", s_stqp[sslot], stq_guard[sslot])
                    vvs = dv("scalar_tensor_tensor", stq[:, sq0:sq0 + 128],
                             bank(bi_, 128), 0.125,
                             sbv8[:, 128 * hp: 128 * hp + 128],
                             ALU.mult, ALU.add)
                    use_bank(bi_, "dv", vvs)
                    rows0 = wd * g + tc * 128
                    wait("sy" if sqq == "sy" else "ac", cdv, vvs)
                    v_stage_last = dma(
                        sqq, s_stqp[sslot],
                        stage_v_bt[bi].ap()[rows0:rows0 + 128,
                                            128 * hpw:128 * (hpw + 1)],
                        stq[:, sq0:sq0 + 128])
                    stq_guard[sslot] = v_stage_last
            if bi == NB2 - 1:
                wait("sy", cdv, vnrm)
                v_nst = dma("sy", s_gn, stage_nrm, qkmax[0:1, 0:2])
                wait("gp", s_gn, v_nst)
                v_cn = coll(stage_nrm, g_nrm_t)
            for si in range(6):
                wait("gp", s_stqp[si], s_stqp[si].v)
            coll(stage_qk_bt[bi].ap(), g_qk_bt[bi])
            v_cb = coll(stage_v_bt[bi].ap(), g_v_bt[bi])
            wait("gp", ccc, v_cb)
            t0b = cs - 2
            emit_recopy(
                myqk_bt[bi].ap().rearrange("a b -> (a b)"),
                g_qk_bt[bi], 2048 * wd, 512 * wd, 512 * wd)
            v_myv = emit_recopy(
                myv_bt[bi].ap().rearrange("a b -> (a b)"),
                g_v_bt[bi], RANKS * wd * 256, wd * 256, wd * 256)

        if phases < 3:
            emit_all()
            return nc
        wait("sy", ccc, v_cn)
        v_gn = dma("sy", s_gn, gnrm_sb[0:1, 0:2 * RANKS],
                   bass.AP(g_nrm_t, 0, [[2 * RANKS, 1], [1, 2 * RANKS]]))
        wait("dv", s_gn, v_gn)
        dv("tensor_max", qkg[0:1, 0:2], gnrm_sb[0:1, 0:2], gnrm_sb[0:1, 2:4])
        dv("tensor_max", qkg2[0:1, 0:2], gnrm_sb[0:1, 4:6],
           gnrm_sb[0:1, 6:8])
        dv("tensor_max", qkg[0:1, 0:2], qkg[0:1, 0:2], qkg2[0:1, 0:2])
        vb1 = dv("tensor_mul", nb_s[0:1, 0:1], qkg[0:1, 0:1], qkg[0:1, 1:2])
        wait("ac", cdv, vb1)
        vb2 = act(nb_s[0:1, 0:1], nb_s[0:1, 0:1], AF.Ln)
        wait("dv", cac, vb2)
        vb3 = dv("tensor_scalar", nb_s[0:1, 0:1], nb_s[0:1, 0:1], 0.5, -LN8,
                 ALU.mult, ALU.add)
        wait("ac", cdv, vb3)
        vb4 = act(nb_s[0:1, 0:1], nb_s[0:1, 0:1], AF.Exp)
        wait("dv", cac, vb4)
        vb5 = dv("tensor_scalar_mul", nb_s[0:1, 0:1], nb_s[0:1, 0:1], -1.0)
        wait("pe", cdv, vb5)
        wait_bank("pe", 2)
        vb6 = mm(bank(2, 1), sonesr[:, :], nb_s[0:1, 0:1], True, True,
                 inc=True)
        wait("dv", cpe, vb6)
        vb7 = dv("tensor_copy", negb[:, :], bank(2, 1))
        use_bank(2, "dv", vb7)

        if phases < 4:
            emit_all()
            return nc
        s_user = {0: ("ac", 0), 1: ("ac", 0)}
        pbuf_user = {0: 0, 1: 0}
        kvb_user = {i: 0 for i in range(4)}
        vzbf_prev = {0: 0, 1: 0}
        v_qblk_user = 0
        wait("ac", cdv, vb7)
        wait("sy", s_cp, v_myv)
        wait("ac", s_cp, v_myv)
        v_vones = 0
        for i in range(4):
            vb0 = 4 * KT + 144 * i
            dv("memset", kvbuf[:, vb0 + 64: vb0 + 65], 1.0)
            v_vones = dv("memset", kvbuf[:, vb0 + 72 + 64: vb0 + 72 + 65], 1.0)
        wait("sy", cdv, v_vones)
        first_att = True

        def emit_pv(kt, nkt, diag0, vsc, vexp, vmask, vpv, zpar):
            par = kt % 2
            idx = kt % 4
            vb0 = 4 * KT + 144 * idx
            zA = 2048 + 1024 * zpar
            wait("pe", cac, vexp[kt])
            if kt >= diag0:
                wait("pe", cpl, vmask[kt])
            if kt == 0:
                wait("pe", cdv, vzbf_prev[zpar])
            first, lastk = kt == 0, kt == nkt - 1
            pa = pbuf[:, par * 2 * QB: par * 2 * QB + QB]
            pb = pbuf[:, par * 2 * QB + QB: (par + 1) * 2 * QB]
            mm(PS[0:65, zA:zA + QB], kvbuf[:, vb0: vb0 + 65],
               pa, first, lastk)
            vpv[kt] = mm(PS[0:65, zA + 512:zA + 512 + QB],
                         kvbuf[:, vb0 + 72: vb0 + 72 + 65],
                         pb, first, lastk, inc=True)
            pbuf_user[par] = vpv[kt]
            kvb_user[idx] = max(kvb_user[idx], vpv[kt], vsc[kt])
        att_i = 0
        ATT = [(hpL, g) for hpL in range(2) for g in range(NQB)]
        v_qb = {}
        v_kv = {}
        qb_user = {0: 0, 1: 0}

        def qb_dma(bidx):
            hpL, g = ATT[bidx]
            rq = (g * QB) // TOK
            qc = (g * QB) % TOK
            bq_ = qc // 512
            par = bidx % 2
            wait("sy", cpe, qb_user[par])
            v_qb[bidx] = dma(
                "sy", s_qbp[par], qblk[:, par * QB:(par + 1) * QB],
                myqk_bt[bq_].ap()[512 * rq + 128 * hpL:
                                  512 * rq + 128 * hpL + 128,
                                  qc % 512: qc % 512 + QB])

        def kv_dma(bidx, kt):
            hpL, g = ATT[bidx]
            idx = kt % 4
            rk = (kt * KT) // TOK
            kc = (kt * KT) % TOK
            kb_ = kc // 512
            kcl = kc % 512
            wait("sy", cpe, kvb_user[idx])
            dma("sy", s_kv[idx], kvbuf[:, idx * KT:(idx + 1) * KT],
                myqk_bt[kb_].ap()[512 * rk + 256 + 128 * hpL:
                                  512 * rk + 256 + 128 * hpL + 128,
                                  kcl:kcl + KT])
            vb0 = 4 * KT + 144 * idx
            wdb = LN2B[kb_][1]
            wait("ac", cpe, kvb_user[idx])
            v_kv[(bidx, kt)] = dma(
                "ac2", s_kv[idx],
                bass.AP(kvbuf, vb0,
                        [[4 * KT + 4 * 144, 128], [72, 2], [1, 64]]),
                myv_bt[kb_].ap()[wdb * rk + kcl: wdb * rk + kcl + KT,
                                 128 * hpL:128 * (hpL + 1)]
                .rearrange("t (h d) -> t h d", h=2))

        qb_dma(0)
        kv_dma(0, 0)
        if (0 * QB + QB) // KT > 1:
            kv_dma(0, 1)
        for bidx, (hpL, g) in enumerate(ATT):
            qpar = bidx % 2
            zpar = bidx % 2
            zA = 2048 + 1024 * zpar
            nkt = (g + 1) * QB // KT
            diag0 = g * QB // KT
            vsc = {}
            vpv = {-1: 0}
            vexp = {}
            vmask = {}
            for kt in range(nkt):
                par = kt % 2
                idx = kt % 4
                sbase = 1024 * par
                who, val = s_user[par]
                wait("pe", CTRS[who], val)
                if kt == 0:
                    wait("pe", s_qbp[qpar], v_qb[bidx])
                    if first_att:
                        for b_ in range(8):
                            wait_bank("pe", b_)
                        first_att = False
                wait("pe", s_kv[idx], v_kv[(bidx, kt)])
                mm(PS[:, sbase:sbase + QB],
                   kvbuf[0:64, idx * KT:(idx + 1) * KT],
                   qblk[0:64, qpar * QB:(qpar + 1) * QB], True, True,
                   tp=(0, 0))
                vsc[kt] = mm(PS[:, sbase + QB:sbase + 2 * QB],
                             kvbuf[64:128, idx * KT:(idx + 1) * KT],
                             qblk[64:128, qpar * QB:(qpar + 1) * QB],
                             True, True, tp=(64, 0), inc=True)
                if kt > 0:
                    emit_pv(kt - 1, nkt, diag0, vsc, vexp, vmask, vpv, zpar)
                wait("ac", cpe, vsc[kt])
                vexp[kt] = act(pbuf[:, par * 2 * QB:(par + 1) * 2 * QB],
                               PS[:, sbase:sbase + 2 * QB], AF.Exp,
                               bias=negb[:, 0:1], scale=0.125, chain=False)
                s_user[par] = ("ac", vexp[kt])
                if kt >= diag0:
                    wait("gp", cac, vexp[kt])
                    vmask[kt] = pl(
                        "affine_select",
                        pbuf[:, par * 2 * QB:(par + 1) * 2 * QB]
                        .rearrange("p (h q) -> p h q", h=2),
                        pbuf[:, par * 2 * QB:(par + 1) * 2 * QB]
                        .rearrange("p (h q) -> p h q", h=2),
                        [[0, 2], [1, QB]], ALU.is_ge, 0.0,
                        base=g * QB - kt * KT, channel_multiplier=-1)
                if kt + 2 < nkt:
                    kv_dma(bidx, kt + 2)
                if kt + 3 == nkt and bidx + 1 < len(ATT):
                    qb_dma(bidx + 1)
                    kv_dma(bidx + 1, 0)
                elif kt + 2 == nkt and bidx + 1 < len(ATT):
                    g2 = ATT[bidx + 1][1]
                    if nkt < 3:
                        qb_dma(bidx + 1)
                        kv_dma(bidx + 1, 0)
                    if (g2 + 1) * QB // KT > 1:
                        kv_dma(bidx + 1, 1)

            emit_pv(nkt - 1, nkt, diag0, vsc, vexp, vmask, vpv, zpar)
            qb_user[qpar] = vsc[nkt - 1]
            if nkt == 1 and bidx + 1 < len(ATT):
                qb_dma(bidx + 1)
                kv_dma(bidx + 1, 0)
                g2 = ATT[bidx + 1][1]
                if (g2 + 1) * QB // KT > 1:
                    kv_dma(bidx + 1, 1)
            wait("dv", cpe, vpv[nkt - 1])
            dv("tensor_copy", scale_sb[0:1, 0:QB],
               PS[64:65, zA:zA + QB])
            vcps = dv("tensor_copy", scale_sb[0:1, QB:2 * QB],
                      PS[64:65, zA + 512:zA + 512 + QB])
            wait("pe", cdv, vcps)
            mm(PS[64:128, zA:zA + QB], sonesr[0:1, 0:64],
               scale_sb[0:1, 0:QB], True, True)
            vbc = mm(PS[64:128, zA + 512:zA + 512 + QB], sonesr[0:1, 0:64],
                     scale_sb[0:1, QB:2 * QB], True, True, inc=True)
            wait("dv", cpe, vbc)
            vrc = dv("reciprocal", scale_sb[0:64, 0:QB],
                     PS[64:128, zA:zA + QB])
            vrc2 = dv("reciprocal", scale_sb[0:64, QB:2 * QB],
                      PS[64:128, zA + 512:zA + 512 + QB])
            dv("tensor_mul", zbf[0:64, 0:QB], PS[0:64, zA:zA + QB],
               scale_sb[0:64, 0:QB])
            vz = dv("tensor_mul", zbf[0:64, QB:2 * QB],
                    PS[0:64, zA + 512:zA + 512 + QB],
                    scale_sb[0:64, QB:2 * QB])
            vzbf_prev[zpar] = vz
            use_bank(4 + 2 * zpar, "dv", vz)
            use_bank(5 + 2 * zpar, "dv", vz)
            wait("sy", cdv, vz)
            dma("sy", s_zst,
                z_dram[64 * (2 * hpL): 64 * (2 * hpL) + 64,
                       QB * g: QB * (g + 1)], zbf[0:64, 0:QB])
            v_zstage = dma(
                "sy", s_zst,
                z_dram[64 * (2 * hpL + 1): 64 * (2 * hpL + 1) + 64,
                       QB * g: QB * (g + 1)], zbf[0:64, QB:2 * QB])
            wait("dv", s_zst, v_zstage)
        for par in range(2):
            who, val = s_user[par]
            use_bank(2 * par, who, val)
            use_bank(2 * par + 1, who, val)

        if phases < 5:
            emit_all()
            return nc
        wait("pe", s_wpj, v_wpr)
        zt_sb = xrb
        HB = TOK // 4
        MB = min(256, HB)
        v_x2add = 0
        for hloc in range(4):
            cs = 2 + HB * hloc
            for mb in range(HB // MB):
                wait("sy", cpe, st_["ffn_pe"])
                wait("sy", s_zst, s_zst.v)
                dma("sy", s_gz, zt_sb[0:64, 0:16 * MB],
                    z_dram[64 * hloc:64 * hloc + 64,
                           16 * MB * mb:16 * MB * (mb + 1)])
                vgz = dma("sy", s_gz, zt_sb[64:128, 0:16 * MB],
                          z_dram[64 * hloc:64 * hloc + 64,
                                 16 * MB * mb:16 * MB * (mb + 1)])
                wait("pe", s_gz, vgz)
                for ec in range(NCH):
                    bA = 4 + (ec % 2)
                    bB = 6 + (ec % 2)
                    wait_bank("pe", bA)
                    wait_bank("pe", bB)
                    for c in range(NCH):
                        first, last = c == 0, c == NCH - 1
                        base = c * E + 128 * ec
                        mm(bank(bA, MB), projw[0:64, base:base + 128],
                           zt_sb[0:64, 2 * c: 2 * c + 16 * (MB - 1) + 1: 16],
                           first, last, tp=(0, 0))
                        vp = mm(bank(bB, MB), projw[64:128, base:base + 128],
                                zt_sb[64:128,
                                      2 * c + 1: 2 * c + 16 * (MB - 1) + 2: 16],
                                first, last, tp=(64, 0), inc=last)
                    ccol = cs + MB * mb
                    wait("dv", cpe, vp)
                    dv("tensor_add",
                       xs[:, ec * COLS + ccol: ec * COLS + ccol + MB],
                       xs[:, ec * COLS + ccol: ec * COLS + ccol + MB],
                       bank(bA, MB))
                    v_x2add = dv("tensor_add",
                                 xs[:, ec * COLS + ccol: ec * COLS + ccol + MB],
                                 xs[:, ec * COLS + ccol: ec * COLS + ccol + MB],
                                 bank(bB, MB))
                    use_bank(bA, "dv", v_x2add)
                    use_bank(bB, "dv", v_x2add)
                st_["ffn_pe"] = max(st_["ffn_pe"], vp)
        st_["xs_dv"] = v_x2add

        v_hst = 0
        wait("sy", cdv, v_x2add)
        for c in range(NCH):
            v_hst = dma("sy", s_zst,
                        stage_x2h[c:c + 1, :],
                        xs[:, c * COLS + COLS - 1: c * COLS + COLS])
        wait("gp", s_zst, v_hst)
        v_ch = coll(stage_x2h, g_x2h_t)
        wait("gp", ccc, v_ch)
        uidh = reg_uid[0]
        reg_uid[0] += 1

        def f_h(G, uid=uidh):
            with G.register(f"ra{uid}") as ra:
                G.reg_load(ra, srnk[0:1, 0:1])
                G.reg_add(ra, ra, 3)
                G.reg_alu(ra, ra, 3, ALU.bitwise_and)
                G.reg_mul(ra, ra, 8 * 128)
                G.dma_start(out=myx2h.rearrange("a b -> (a b)"),
                            in_=bass.AP(g_x2h_t, ra, [[1, 8 * 128]])
                            ).then_inc(s_cp.sem, 16)
        on("gp", f_h)
        s_cp.bump()
        v_hcp = s_cp.v
        wait("sy", s_cp, v_hcp)
        v_hld = 0
        for c in range(NCH):
            v_hld = dma("sy", s_gz, xs[:, c * COLS + 1: c * COLS + 2],
                        myx2h[c:c + 1, :])
        v_s3 = emit_ln_stats(*LN3B[1])
        v_b3 = emit_ln_rstd(*LN3B[1], v_s3)
        emit_ln_apply(*LN3B[1], v_b3)
        wait("dv", s_gz, v_hld)
        wait("ac", s_gz, v_hld)
        wait("pe", s_gz, v_hld)
        vs_h = emit_ln_stats(1, 1, yb=ylast, ystride=1)
        vb_h = emit_ln_rstd(1, 1, vs_h)
        emit_ln_apply(1, 1, vb_h, halo_mask=True, yb=ylast, ystride=1)

        v_tr_dv = 0
        v_outst = 0
        for bi in range(1, len(LN3B)):
            cs, wd = LN3B[bi]
            nxt = LN3B[bi + 1] if bi + 1 < len(LN3B) else None
            v_x3 = emit_ffn_block(cs, wd, cs, next_ln=nxt)
            ntc = wd // 128
            for tc in range(ntc):
                c0 = cs + tc * 128
                trow = c0 - 2
                for half in range(2):
                    bi_ = 6 + half
                    wait_bank("pe", bi_)
                    wait("pe", cdv, v_x3)
                    for j in range(4):
                        ecp = 4 * half + j
                        vt = trans(
                            PS[:, 512 * bi_ + 128 * j:
                               512 * bi_ + 128 * j + 128],
                            xs[:, ecp * COLS + c0: ecp * COLS + c0 + 128],
                            inc=(j == 3))
                    wait("ac", cpe, vt)
                    wait("ac", s_out, v_outst)
                    v_tr_ac = act(outsb[:, :], bank(bi_, 512), AF.Copy,
                                  chain=False)
                    use_bank(bi_, "ac", v_tr_ac)
                    wait("ac", cac, v_tr_ac)
                    v_outst = dma("sc", s_out,
                                  out[trow:trow + 128,
                                      512 * half:512 * (half + 1)],
                                  outsb[:, :])

        on("ac", lambda A, v=s_out.v: A.wait_ge(s_out.sem, v))

        emit_all()
    return nc


def _noop():
    pass


def _build_emit(block, ops):
        @block.sync
        def _(sync):
            for fn in ops["sy"]:
                fn(sync)

        @block.gpsimd
        def _(gpsimd):
            for fn in ops["gp"]:
                fn(gpsimd)

        @block.tensor
        def _(tensor):
            for fn in ops["pe"]:
                fn(tensor)

        @block.scalar
        def _(scalar):
            for fn in ops["ac"]:
                fn(scalar)

        @block.vector
        def _(vector):
            for fn in ops["dv"]:
                fn(vector)


def make_inputs(x_full, Wq, bq, Wk, bk, Wv, bv, proj, W_r, W_k, W_v, mu_r,
                mu_k, TOK=2048):
    f32 = np.float32
    WqT = np.ascontiguousarray(np.asarray(Wq).reshape(H * DH, E).T.astype(f32))
    WkTa = np.ascontiguousarray(np.asarray(Wk).reshape(H * DH, E).T.astype(f32))
    WvTa = np.ascontiguousarray(np.asarray(Wv).reshape(H * DH, E).T.astype(f32))
    wpr = np.ascontiguousarray(np.asarray(proj).astype(f32))
    wrT = np.ascontiguousarray(np.asarray(W_r).T.astype(f32))
    wkT = np.ascontiguousarray(np.asarray(W_k).T.astype(f32))
    wvT = np.ascontiguousarray(np.asarray(W_v).T.astype(f32))
    bq_t = np.ascontiguousarray(np.asarray(bq).reshape(8, 128).T.astype(f32))
    bk_t = np.ascontiguousarray(np.asarray(bk).reshape(8, 128).T.astype(f32))
    bv_b = np.ascontiguousarray(np.broadcast_to(
        np.asarray(bv).reshape(1, H * DH), (128, H * DH)).astype(f32))
    mur_t = np.ascontiguousarray(np.asarray(mu_r).reshape(8, 128).T.astype(f32))
    muk_t = np.ascontiguousarray(np.asarray(mu_k).reshape(8, 128).T.astype(f32))
    id_ = np.eye(128, dtype=f32)
    onesr_ = np.ones((1, 128), dtype=f32)
    onescb_ = np.ones((128, 1), dtype=ml_dtypes.bfloat16)
    consts_ = np.array([[1e-5, 0.0, 0.0, 0.0]], dtype=f32)

    in_maps = []
    for core in range(8):
        b, i = core // RANKS, core % RANKS
        s0 = i * TOK
        xsl = np.zeros((TOK + 2, E), dtype=f32)
        lo = max(0, s0 - 2)
        xsl[2 - (s0 - lo):] = x_full[b, lo:s0 + TOK]
        xin_ = np.ascontiguousarray(xsl.T)
        hm = np.full((128, 1), 0.0 if i == 0 else 1.0, dtype=f32)
        in_maps.append(dict(
            xin=xin_, wrT=wrT, wkT=wkT, wvT=wvT, wqaT=WqT, wkaT=WkTa,
            wvaT=WvTa, wpr=wpr, bq_t=bq_t, bk_t=bk_t, bv_bc=bv_b,
            mur_t=mur_t, muk_t=muk_t, hmask=hm, id128=id_, onesr=onesr_,
            onescb=onescb_, consts=consts_,
            rnk=np.array([[i]], dtype=np.int32)))
    return in_maps


def kernel(L0_attn_x, L0_attn_Wq, L0_attn_bq, L0_attn_Wk, L0_attn_bk,
           L0_attn_Wv, L0_attn_bv, L0_attn_proj, L0_ffn_W_r, L0_ffn_W_k,
           L0_ffn_W_v, L0_ffn_mu_r, L0_ffn_mu_k):
    TOK = 2048
    x = np.asarray(L0_attn_x, dtype=np.float32)
    in_maps = make_inputs(
        x, L0_attn_Wq, L0_attn_bq, L0_attn_Wk, L0_attn_bk, L0_attn_Wv,
        L0_attn_bv, L0_attn_proj, L0_ffn_W_r, L0_ffn_W_k, L0_ffn_W_v,
        L0_ffn_mu_r, L0_ffn_mu_k, TOK=TOK)
    nc = build_nc(TOK=TOK, QB=512)
    res = run_bass_kernel_spmd(nc, in_maps, core_ids=list(range(8)))
    outf = np.empty((2, RANKS * TOK, E), dtype=np.float32)
    for core in range(8):
        b, i = core // RANKS, core % RANKS
        outf[b, i * TOK:(i + 1) * TOK] = res.results[core]["out"]
    return outf
